# version 05
# baseline (speedup 1.0000x reference)
import contextlib
import numpy as np
import concourse.bass as bass
import concourse.mybir as mybir
from concourse.bass_utils import run_bass_kernel_spmd

F32 = mybir.dt.float32
BF16 = mybir.dt.bfloat16
AF = mybir.ActivationFunctionType
ALU = mybir.AluOpType
AX = mybir.AxisListType.X

H = 1024
NH = 16
TC = 2048
NT = 16
NP = 48
NE = 9
IO = 2048
NG = 4
NCORE = 8
EPS_LN = 1e-5
EPS_AT = 1e-12
RS2 = float(1.0 / np.sqrt(2.0))
SQ_SCALE = float(1.0 / (2.0 * np.sqrt(2.0)))

_PHASES = "full"
_DEBUG = False
_TRUNC = None
_LAST_S = None

OFF_PROD = 16
OFF_DIAG = 136
COL_ONES = 152
DPHI = 160


class Sched:
    ROLES = ("sync", "scalar", "gpsimd", "tensor", "vector")

    def __init__(self):
        self.ops = {r: [] for r in self.ROLES}
        self.cnt = {s: 0 for s in ("qX", "qT", "qW", "sP", "sV", "sA")}
        self.gidx = 0

    def do(self, role, fn, waits=(), inc=None):
        inc_by = None
        if inc is not None:
            inc_by = 16 if inc.startswith("q") else 1
            self.cnt[inc] += inc_by
        waits = tuple((s, v) for (s, v) in waits if v > 0)

        def run(eng, _fn=fn, _waits=waits, _inc=inc, _by=inc_by):
            for (s, v) in _waits:
                eng.wait_ge(self._h[s], v)
            ins = _fn(eng)
            if _inc is not None:
                ins.then_inc(self._h[_inc], _by)
        self.ops[role].append((self.gidx, run))
        self.gidx += 1
        return self.cnt[inc] if inc is not None else None

    def wait(self, role, sem, val):
        if val <= 0:
            return

        def run(eng, _s=sem, _v=val):
            eng.wait_ge(self._h[_s], _v)
        self.ops[role].append((self.gidx, run))
        self.gidx += 1

    def bind(self, handles):
        self._h = handles


def build_program(debug=False, phases="full"):
    global _LAST_S
    nc = bass.Bass("TRN2", target_bir_lowering=False)
    S = Sched()
    _LAST_S = S
    mark = {}
    M = mark.get

    d_x = nc.dram_tensor("x_main", [TC, H], F32, kind="ExternalInput").ap()
    d_xp = nc.dram_tensor("x_pre", [NP * 128, H], F32, kind="ExternalInput").ap()
    d_pmh = nc.dram_tensor("pmask_h", [128, NP], F32, kind="ExternalInput").ap()
    d_pm1 = nc.dram_tensor("pmask_1", [128, NP], F32, kind="ExternalInput").ap()
    d_cm = nc.dram_tensor("cmask", [128, 128], F32, kind="ExternalInput").ap()
    d_wq = nc.dram_tensor("wqT", [H, 256], F32, kind="ExternalInput").ap()
    d_wk = nc.dram_tensor("wkT", [H, 512], F32, kind="ExternalInput").ap()
    d_wv = nc.dram_tensor("wvT", [H, H], F32, kind="ExternalInput").ap()
    d_wo = nc.dram_tensor("woT", [H, H], F32, kind="ExternalInput").ap()
    d_rw = nc.dram_tensor("rwm", [128, 64], F32, kind="ExternalInput").ap()
    d_lnw = nc.dram_tensor("lnw", [128, 8], F32, kind="ExternalInput").ap()
    d_lnb = nc.dram_tensor("lnb", [128, 8], F32, kind="ExternalInput").ap()
    d_gw = nc.dram_tensor("gw", [NE, H, IO], F32, kind="ExternalInput").ap()
    d_uw = nc.dram_tensor("uw", [NE, H, IO], F32, kind="ExternalInput").ap()
    d_dw = nc.dram_tensor("dw", [NE, IO, H], F32, kind="ExternalInput").ap()
    d_out = nc.dram_tensor("out", [TC, H], F32, kind="ExternalOutput").ap()
    if debug:
        d_dx2 = nc.dram_tensor("dbg_x2", [TC, H], F32, kind="ExternalOutput").ap()
        d_dnf = nc.dram_tensor("dbg_nf", [H, TC], F32, kind="ExternalOutput").ap()
        d_dwr = nc.dram_tensor("dbg_wrt", [128, NE * NT], F32, kind="ExternalOutput").ap()

    with contextlib.ExitStack() as top:
        sb0 = lambda name, shape, dt: top.enter_context(nc.sbuf_tensor(name, shape, dt))
        c_pmh = sb0("c_pmh", [128, NP], F32)
        c_pm1 = sb0("c_pm1", [128, NP], F32)
        c_cm = sb0("c_cm", [128, 128], BF16)
        c_one1 = sb0("c_one1", [1, 128], BF16)
        b_st = sb0("b_st", [128, 2, 16], F32)
        b_sq = sb0("b_sq", [128, 512], F32)
        w_rt = sb0("w_rt", [128, NE, NT], F32)
        b_mx = sb0("b_mx", [128, NT], F32)
        b_eq = sb0("b_eq", [128, NT, 8], F32)
        rw_f = sb0("rw_f", [128, 8, 8], F32)
        rw_h = sb0("rw_h", [128, 8, 8], BF16)
        rw_l = sb0("rw_l", [128, 8, 8], BF16)
        rw_raw = sb0("rw_raw", [128, 8, 8], F32)
        rwr_h = sb0("rwr_h", [128, 8, 8], BF16)
        rwr_l = sb0("rwr_l", [128, 8, 8], BF16)
        lnwb = sb0("lnwb", [128, 2, 8], F32)
        lnb_h = sb0("lnb_h", [128, 8], BF16)
        lnb_l = sb0("lnb_l", [128, 8], BF16)
        c_pc = sb0("c_pc", [128, 2], F32)
        brw_f = sb0("brw_f", [1, 8], F32)
        brw_h = sb0("brw_h", [1, 8], BF16)
        brw_l = sb0("brw_l", [1, 8], BF16)
        pb = [top.enter_context(nc.psum_tensor(f"pb{i}", [128, 512], F32)) for i in range(8)]

        S.do("scalar", lambda e: e.dma_start(out=c_pmh[:, :], in_=d_pmh), inc="qX")
        S.do("scalar", lambda e: e.dma_start(out=c_pm1[:, :], in_=d_pm1), inc="qX")
        S.do("gpsimd", lambda e: e.dma_start(out=c_cm[:, :], in_=d_cm), inc="qW")
        S.do("scalar", lambda e: e.dma_start(out=lnwb[:, 0, :], in_=d_lnw), inc="qX")
        S.do("scalar", lambda e: e.dma_start(out=lnwb[:, 1, :], in_=d_lnb), inc="qX")
        S.do("scalar", lambda e: e.dma_start(
            out=rw_f[:, :, :].rearrange("p c e -> p (c e)"), in_=d_rw), inc="qX")
        mk_rwraw = S.do("scalar", lambda e: e.dma_start(
            out=rw_raw[:, :, :].rearrange("p c e -> p (c e)"), in_=d_rw), inc="qX")
        mk_consts = S.cnt["qX"]

        S.do("vector", lambda e: e.memset(c_one1[:, :], 1.0))
        S.do("vector", lambda e: e.memset(c_pc[:, 0:1], SQ_SCALE))
        S.do("vector", lambda e: e.memset(c_pc[:, 1:2], RS2))
        S.wait("vector", "qX", mk_consts)
        mkrwf = None
        for c in range(8):
            mkrwf = S.do("vector", lambda e, _c=c: e.tensor_scalar(
                out=rw_f[:, _c, :], in0=rw_f[:, _c, :],
                scalar1=lnwb[:, 0, _c].unsqueeze(1), scalar2=None, op0=ALU.mult),
                inc="sV" if c == 7 else None)
        S.do("vector", lambda e: e.tensor_copy(out=lnb_h[:, :], in_=lnwb[:, 1, :]))
        mkh1 = S.do("vector", lambda e: e.tensor_copy(
            out=rw_h[:, :, :], in_=rw_f[:, :, :]), waits=[("sV", mkrwf)], inc="sV")
        S.do("vector", lambda e: e.tensor_copy(out=rwr_h[:, :, :], in_=rw_raw[:, :, :]),
             waits=[("sV", mkh1)])
        S.do("vector", lambda e: e.scalar_tensor_tensor(
            out=rw_l[:, :, :], in0=rw_h[:, :, :], scalar=-1.0, in1=rw_f[:, :, :],
            op0=ALU.mult, op1=ALU.add))
        S.do("vector", lambda e: e.scalar_tensor_tensor(
            out=lnb_l[:, :], in0=lnb_h[:, :], scalar=-1.0, in1=lnwb[:, 1, :],
            op0=ALU.mult, op1=ALU.add))
        mk_split = S.do("vector", lambda e: e.scalar_tensor_tensor(
            out=rwr_l[:, :, :], in0=rwr_h[:, :, :], scalar=-1.0, in1=rw_raw[:, :, :],
            op0=ALU.mult, op1=ALU.add), inc="sV")
        S.do("vector", lambda e: e.memset(w_rt[:, 8, :], 1.0))
        S.wait("tensor", "sV", mk_split)
        nmm = [0]
        for (lh, rh) in ((lnb_h, rwr_h), (lnb_h, rwr_l), (lnb_l, rwr_h)):
            for c in range(8):
                nmm[0] += 1
                S.do("tensor", lambda e, _l=lh, _r=rh, _c=c, _n=nmm[0]:
                     e.matmul(out=pb[1][0:1, 0:8], lhsT=_l[:, _c].unsqueeze(1),
                              rhs=_r[:, _c, :], start=(_n == 1), stop=(_n == 24),
                              skip_group_check=True),
                     inc="sP" if nmm[0] == 24 else None)
        mk_brwmm = S.cnt["sP"]
        S.wait("vector", "sP", mk_brwmm)
        mkb1 = S.do("vector", lambda e: e.tensor_copy(
            out=brw_f[0:1, :], in_=pb[1][0:1, 0:8]), inc="sV")
        mkb2 = S.do("vector", lambda e: e.tensor_copy(
            out=brw_h[0:1, :], in_=brw_f[0:1, :]), waits=[("sV", mkb1)], inc="sV")
        mk_brw = S.do("vector", lambda e: e.scalar_tensor_tensor(
            out=brw_l[0:1, :], in0=brw_h[0:1, :], scalar=-1.0, in1=brw_f[0:1, :],
            op0=ALU.mult, op1=ALU.add), waits=[("sV", mkb2)], inc="sV")

        with contextlib.ExitStack() as mid:
            x2 = mid.enter_context(nc.sbuf_tensor("x2", [128, NT, H], F32))

            with contextlib.ExitStack() as att:
                sba = lambda name, shape, dt: att.enter_context(nc.sbuf_tensor(name, shape, dt))
                w_q = sba("w_q", [128, 8, 256], BF16)
                w_k = sba("w_k", [128, 8, 512], BF16)
                w_v = sba("w_v", [128, 8, H], BF16)
                w_o = sba("w_o", [128, 8, H], BF16)
                b_x = sba("b_x", [128, 3, H], F32)
                b_ln1 = sba("b_ln1", [128, 2, H], BF16)
                b_lf = sba("b_lf", [128, 2, 8, 128], BF16)
                b_qh = sba("b_qh", [128, 2, 256], F32)
                b_kh = sba("b_kh", [128, 2, 512], F32)
                b_kb = sba("b_kb", [128, 2048], BF16)
                b_kf = sba("b_kf", [128, 2, 16, 128], BF16)
                b_va = sba("b_va", [128, 2, NH, 65], BF16)
                b_fq = sba("b_fq", [128, NH, 256], BF16)
                b_PH = sba("b_PH", [128, 2, 32, 128], BF16)
                b_fk = sba("b_fk", [128, 2, NH, DPHI], BF16)
                b_A = sba("b_A", [128, 2, 128], BF16)
                b_ps = sba("b_ps", [128, 2, 128], F32)
                b_y = sba("b_y", [128, H], BF16)
                b_yf = sba("b_yf", [128, 8, 128], BF16)
                KV1 = sba("KV1", [128, NH, 65], F32)
                KV2 = sba("KV2", [32, NH, 65], F32)
                KV1b = sba("KV1b", [128, NH, 65], BF16)
                KV2b = sba("KV2b", [32, NH, 65], BF16)

                for (dst, src) in ((w_q, d_wq), (w_k, d_wk), (w_v, d_wv), (w_o, d_wo)):
                    S.do("gpsimd", lambda e, _d=dst, _s=src: e.dma_start(
                        out=_d[:, :, :], in_=_s.rearrange("(c p) n -> p c n", p=128)),
                        inc="qW")
                mk_wattn = S.cnt["qW"]

                S.do("vector", lambda e: e.memset(b_fq[:, :, COL_ONES:256], 0.0))
                S.do("vector", lambda e: e.memset(b_fq[:, :, COL_ONES], 1.0))
                S.do("vector", lambda e: e.memset(b_fk[:, :, :, 153:DPHI], 0.0))
                S.do("vector", lambda e: e.memset(KV1[:, :, :], 0.0))
                mk_kvinit = S.do("vector", lambda e: e.memset(KV2[:, :, :], 0.0), inc="sV")

                def emit_ln_stats(src_fn, par):
                    for hv in range(2):
                        S.do("scalar", lambda e, _h=hv, _p=par: e.activation(
                            out=b_sq[:, :], in_=src_fn(_h), func=AF.Copy,
                            accum_out=b_st[:, _p, 0 + _h].unsqueeze(1)))
                    mkA = None
                    for hv in range(2):
                        mkA = S.do("scalar", lambda e, _h=hv, _p=par: e.activation(
                            out=b_sq[:, :], in_=src_fn(_h), func=AF.Square,
                            accum_out=b_st[:, _p, 2 + _h].unsqueeze(1)),
                            inc="sA" if hv == 1 else None)
                    S.wait("vector", "sA", mkA)
                    S.do("vector", lambda e, _p=par: e.tensor_scalar(
                        out=b_st[:, _p, 4].unsqueeze(1), in0=b_st[:, _p, 0].unsqueeze(1),
                        scalar1=b_st[:, _p, 1].unsqueeze(1), scalar2=1.0 / H,
                        op0=ALU.add, op1=ALU.mult))
                    mk1 = S.do("vector", lambda e, _p=par: e.tensor_scalar(
                        out=b_st[:, _p, 7].unsqueeze(1), in0=b_st[:, _p, 2].unsqueeze(1),
                        scalar1=b_st[:, _p, 3].unsqueeze(1), scalar2=1.0 / H,
                        op0=ALU.add, op1=ALU.mult), inc="sV")
                    mk2 = S.do("vector", lambda e, _p=par: e.tensor_scalar(
                        out=b_st[:, _p, 6].unsqueeze(1), in0=b_st[:, _p, 4].unsqueeze(1),
                        scalar1=b_st[:, _p, 4].unsqueeze(1), scalar2=None, op0=ALU.mult),
                        waits=[("sV", mk1)], inc="sV")
                    mk3 = S.do("vector", lambda e, _p=par: e.tensor_scalar(
                        out=b_st[:, _p, 8].unsqueeze(1), in0=b_st[:, _p, 7].unsqueeze(1),
                        scalar1=b_st[:, _p, 6].unsqueeze(1), scalar2=EPS_LN,
                        op0=ALU.subtract, op1=ALU.add), waits=[("sV", mk2)], inc="sV")
                    mkV = S.do("vector", lambda e, _p=par: e.reciprocal(
                        out=b_st[:, _p, 9].unsqueeze(1), in_=b_st[:, _p, 8].unsqueeze(1)),
                        waits=[("sV", mk3)], inc="sV")
                    S.wait("scalar", "sV", mkV)
                    return S.do("scalar", lambda e, _p=par: e.activation(
                        out=b_st[:, _p, 5].unsqueeze(1), in_=b_st[:, _p, 9].unsqueeze(1),
                        func=AF.Sqrt), inc="sA")

                NTOT = NP + NT

                def keyof(gi):
                    return ("p", gi) if gi < NP else ("m", gi - NP)

                def Mg(name, gi):
                    if gi < 0:
                        return 0
                    return M((name,) + keyof(gi), 0)

                def phi_ops(src_fn, dst_fn, waits0, inc=None):
                    for j in range(1, 16):
                        off = OFF_PROD + j * (j - 1) // 2
                        sv = src_fn(j, 1)
                        S.do("vector", lambda e, _j=j, _o=off, _sv=sv: e.tensor_tensor(
                            out=dst_fn(_o, _j), in0=src_fn(0, _j),
                            in1=_sv.broadcast_to(tuple(_sv.shape[0:2]) + (_j,)),
                            op=ALU.mult), waits=(waits0 if j == 1 else ()))
                    S.do("vector", lambda e: e.scalar_tensor_tensor(
                        out=dst_fn(OFF_DIAG, 16), in0=src_fn(0, 16), scalar=RS2,
                        in1=src_fn(0, 16), op0=ALU.mult, op1=ALU.mult))
                    return S.do("vector", lambda e: e.tensor_copy(
                        out=dst_fn(0, 16), in_=src_fn(0, 16)), inc=inc)

                def emit_front(gi):
                    prefix = gi < NP
                    t = gi if prefix else gi - NP
                    b = gi % 2
                    bx = gi % 3
                    key = keyof(gi)
                    x_src = (d_xp if prefix else d_x).rearrange("(n p) m -> n p m", p=128)[t]
                    S.do("scalar", lambda e, _s=x_src, _b=bx: e.dma_start(
                        out=b_x[:, _b, :], in_=_s),
                        waits=[("sV", Mg("xdone", gi - 3))], inc="qX")
                    mk_xld = S.cnt["qX"]
                    S.wait("scalar", "qX", mk_xld)
                    mkS = emit_ln_stats(
                        lambda hv, _b=bx: b_x[:, _b, hv * 512:(hv + 1) * 512], b)
                    S.wait("vector", "sA", mkS)
                    mk_ln1 = S.do("vector", lambda e, _b=b, _bx=bx: e.tensor_scalar(
                        out=b_ln1[:, _b, :], in0=b_x[:, _bx, :],
                        scalar1=b_st[:, _b, 4].unsqueeze(1),
                        scalar2=b_st[:, _b, 5].unsqueeze(1),
                        op0=ALU.subtract, op1=ALU.mult),
                        waits=[("qT", Mg("ln1T", gi - 2))], inc="sV")
                    mark[("ln1T",) + key] = S.do("sync", lambda e, _b=b: e.dma_start(
                        out=b_lf[:, _b, :, :],
                        in_=b_ln1[:, _b, :].rearrange("p (B i) -> p B i", B=8),
                        transpose=True),
                        waits=[("sV", mk_ln1), ("sP", Mg("projv", gi - 2))], inc="qT")
                    mk_lnT = S.cnt["qT"]
                    pe_w = [("qT", mk_lnT), ("qW", mk_wattn), ("sV", mk_brw),
                            ("sA", Mg("khdone", gi - 1)),
                            ("sV", Mg("ktm", gi - 1))]
                    if prefix:
                        pe_w.append(("sV", Mg("vadone", gi - 1)))
                    mms = []
                    for c in range(8):
                        if not prefix:
                            mms.append((c, pb[1], 256, w_q, 0))
                        mms.append((c, pb[0], 512, w_k, 0))
                        if prefix:
                            mms.append((c, pb[2], 512, w_v, 0))
                            mms.append((c, pb[3], 512, w_v, 512))
                    for i, (c, bank, n, ww, c0) in enumerate(mms):
                        S.do("tensor", lambda e, _c=c, _b=b, _bk=bank, _n=n, _w=ww, _c0=c0:
                             e.matmul(out=_bk[:, 0:_n], lhsT=b_lf[:, _b, _c, :],
                                      rhs=_w[:, _c, _c0:_c0 + _n],
                                      start=(_c == 0), stop=(_c == 7),
                                      skip_group_check=True),
                             waits=pe_w if i == 0 else (),
                             inc="sP" if i == len(mms) - 1 else None)
                    mark[("projqk",) + key] = S.cnt["sP"]
                    if prefix:
                        mark[("projv",) + key] = S.cnt["sP"]

                def emit_vproj(gi):
                    b = gi % 2
                    key = keyof(gi)
                    pe_w = [("sV", Mg("x2add", gi - 1)), ("sV", Mg("vadone", gi - 1))]
                    n3 = 0
                    for c in range(8):
                        for hf in range(2):
                            n3 += 1
                            S.do("tensor", lambda e, _c=c, _b=b, _hf=hf: e.matmul(
                                out=pb[2 + _hf][:, 0:512], lhsT=b_lf[:, _b, _c, :],
                                rhs=w_v[:, _c, _hf * 512:(_hf + 1) * 512],
                                start=(_c == 0), stop=(_c == 7), skip_group_check=True),
                                waits=pe_w if n3 == 1 else (),
                                inc="sP" if n3 == 16 else None)
                    mark[("projv",) + key] = S.cnt["sP"]

                def emit_body(gi):
                    prefix = gi < NP
                    t = gi if prefix else gi - NP
                    b = gi % 2
                    bx = gi % 3
                    key = keyof(gi)
                    mk_projqk = mark[("projqk",) + key]
                    mk_projv = mark[("projv",) + key]

                    if not prefix:
                        S.do("scalar", lambda e, _b=b: e.activation(
                            out=b_qh[:, _b, :], in_=pb[1][:, 0:256], func=AF.Copy,
                            scale=0.5),
                            waits=[("sP", mk_projqk), ("sV", Mg("fq", gi - 2))])
                    if prefix:
                        kh_fn = lambda e, _t=t, _b=b: e.activation(
                            out=b_kh[:, _b, :], in_=pb[0][:, 0:512], func=AF.Copy,
                            scale=c_pmh[:, _t].unsqueeze(1))
                    else:
                        kh_fn = lambda e, _b=b: e.activation(
                            out=b_kh[:, _b, :], in_=pb[0][:, 0:512], func=AF.Copy,
                            scale=0.5)
                    mark[("khdone",) + key] = S.do(
                        "scalar", kh_fn,
                        waits=[("sP", mk_projqk), ("sV", Mg("fk", gi - 2))], inc="sA")
                    mk_kh = S.cnt["sA"]
                    if not prefix:
                        mk_ktm = S.do("vector", lambda e: e.tensor_copy(
                            out=b_kb[:, :].rearrange("p (h n) -> p h n", h=16)[:, :, 0:32],
                            in_=pb[0][:, 0:512].rearrange("p (h n) -> p h n", h=16)),
                            waits=[("sP", mk_projqk), ("sA", mk_kh),
                                   ("qT", Mg("kT", gi - 1))], inc="sV")
                        mark[("ktm",) + key] = mk_ktm
                        mark[("kT",) + key] = S.do("sync", lambda e, _b=b: e.dma_start(
                            out=b_kf[:, _b, :, :],
                            in_=b_kb[:, :].rearrange("p (B i) -> p B i", B=16),
                            transpose=True),
                            waits=[("sV", mk_ktm), ("sP", Mg("state", gi - 2))], inc="qT")
                    else:
                        mark[("ktm",) + key] = 0
                    S.do("vector", lambda e, _b=b: e.tensor_copy(
                        out=b_va[:, _b, 0:8, 0:64],
                        in_=pb[2][:, 0:512].rearrange("p (h n) -> p h n", h=8)),
                        waits=[("sP", mk_projv), ("sP", Mg("state", gi - 2))])
                    S.do("vector", lambda e, _b=b: e.tensor_copy(
                        out=b_va[:, _b, 8:16, 0:64],
                        in_=pb[3][:, 0:512].rearrange("p (h n) -> p h n", h=8)))
                    if prefix:
                        ones_fn = lambda e, _b=b, _t=t: e.tensor_copy(
                            out=b_va[:, _b, :, 64].unsqueeze(2),
                            in_=c_pm1[:, _t].unsqueeze(1).unsqueeze(2).broadcast_to(
                                [128, NH, 1]))
                    else:
                        ones_fn = lambda e, _b=b: e.memset(b_va[:, _b, :, 64], 1.0)
                    mark[("vadone",) + key] = S.do("vector", ones_fn, inc="sV")
                    mk_va = S.cnt["sV"]

                    def part2():
                        if not prefix:
                            fq_src = lambda i0, ln, _b=b: b_qh[:, _b, :].rearrange(
                                "p (h i) -> p h i", h=16)[:, :, i0:i0 + ln]
                            fq_dst = lambda o, ln: b_fq[:, :, o:o + ln]
                            mk_fq = phi_ops(fq_src, fq_dst,
                                            [("sA", mk_kh), ("qT", Mg("fqT", gi - 1))],
                                            inc="sV")
                            mark[("fq",) + key] = mk_fq
                            mark[("fqT",) + key] = S.do("sync", lambda e, _b=b: e.dma_start(
                                out=b_PH[:, _b, :, :],
                                in_=b_fq[:, :, :].rearrange("p h i -> p (h i)").rearrange(
                                    "p (B i) -> p B i", B=32), transpose=True),
                                waits=[("sV", mk_fq), ("sP", Mg("state", gi - 2))],
                                inc="qT")
                        fk_src = lambda i0, ln, _b=b: b_kh[:, _b, :].rearrange(
                            "p (h i) -> p h i", h=16)[:, :, i0:i0 + ln]
                        fk_dst = lambda o, ln, _b=b: b_fk[:, _b, :, o:o + ln]
                        phi_ops(fk_src, fk_dst,
                                [("sA", mk_kh), ("sP", Mg("state", gi - 2))])
                        if prefix:
                            mk_fk = S.do("vector", lambda e, _b=b, _t=t: e.tensor_copy(
                                out=b_fk[:, _b, :, COL_ONES].unsqueeze(2),
                                in_=c_pm1[:, _t].unsqueeze(1).unsqueeze(2).broadcast_to(
                                    [128, NH, 1])), inc="sV")
                        else:
                            mk_fk = S.do("vector", lambda e, _b=b: e.memset(
                                b_fk[:, _b, :, COL_ONES], 1.0), inc="sV")
                        mark[("fk",) + key] = mk_fk

                        if not prefix:
                            S.do("vector", lambda e: e.tensor_copy(
                                out=KV1b[:, :, :], in_=KV1[:, :, :]),
                                waits=[("sP", Mg("Ylast", gi - 1))])
                            mk_kvb = S.do("vector", lambda e: e.tensor_copy(
                                out=KV2b[:, :, :], in_=KV2[:, :, :]), inc="sV")
                            mk_kT = mark[("kT",) + key]
                            mk_fqT = mark[("fqT",) + key]

                            def emit_score(h):
                                w = [("sV", M(("poly", t, h - 2), 0))]
                                if h == 0:
                                    w += [("qT", mk_kT), ("qT", mk_fqT),
                                          ("sV", Mg("stadd", gi - 1))]
                                S.do("tensor", lambda e, _b=b, _h=h: e.matmul(
                                    out=pb[4 + _h % 2][:, 0:128],
                                    lhsT=b_kf[0:16, _b, _h, :],
                                    rhs=b_PH[0:16, _b, 2 * _h, :],
                                    start=True, stop=True, tile_position=(0, 0),
                                    skip_group_check=True),
                                    waits=w, inc="sP")
                                mark[("score", t, h)] = S.cnt["sP"]
                                S.do("scalar", lambda e, _h=h: e.activation(
                                    out=b_ps[:, _h % 2, :], in_=pb[4 + _h % 2][:, 0:128],
                                    func=AF.Square, scale=c_pc[:, 0].unsqueeze(1),
                                    bias=c_pc[:, 1].unsqueeze(1)),
                                    waits=[("sP", mark[("score", t, h)]),
                                           ("sV", M(("poly", t, h - 2), 0))], inc="sA")
                                mark[("asq", t, h)] = S.cnt["sA"]
                                S.do("vector", lambda e, _h=h: e.scalar_tensor_tensor(
                                    out=b_A[:, _h % 2, :], in0=b_ps[:, _h % 2, :],
                                    scalar=0.5, in1=c_cm[:, :],
                                    op0=ALU.add, op1=ALU.mult),
                                    waits=[("sA", mark[("asq", t, h)]),
                                           ("sP", M(("Ymm", t, h - 2), 0))], inc="sV")
                                mark[("poly", t, h)] = S.cnt["sV"]

                            def emit_Y(h):
                                wy = [("sV", mark[("poly", t, h)]),
                                      ("sV", M(("norm", t, h - 2), 0))]
                                if h == 0:
                                    wy += [("sV", mk_kvb), ("sV", mk_va)]
                                S.do("tensor", lambda e, _b=b, _h=h: e.matmul(
                                    out=pb[6 + _h % 2][:, 0:65],
                                    lhsT=b_PH[:, _b, 2 * _h, :],
                                    rhs=KV1b[:, _h, :], start=True, stop=False,
                                    skip_group_check=True), waits=wy)
                                S.do("tensor", lambda e, _b=b, _h=h: e.matmul(
                                    out=pb[6 + _h % 2][:, 0:65],
                                    lhsT=b_PH[0:32, _b, 2 * _h + 1, :],
                                    rhs=KV2b[:, _h, :], start=False, stop=False,
                                    skip_group_check=True))
                                S.do("tensor", lambda e, _b=b, _h=h: e.matmul(
                                    out=pb[6 + _h % 2][:, 0:65], lhsT=b_A[:, _h % 2, :],
                                    rhs=b_va[:, _b, _h, :], start=False, stop=True,
                                    skip_group_check=True), inc="sP")
                                mark[("Ymm", t, h)] = S.cnt["sP"]
                                wn = [("sP", mark[("Ymm", t, h)])]
                                if h == 0:
                                    wn.append(("qT", Mg("yT", gi - 1)))
                                mkn1 = S.do("vector", lambda e, _h=h: e.tensor_scalar(
                                    out=b_st[:, 0, 10].unsqueeze(1),
                                    in0=pb[6 + _h % 2][:, 64].unsqueeze(1),
                                    scalar1=EPS_AT, scalar2=None, op0=ALU.add),
                                    waits=wn, inc="sV")
                                mkn2 = S.do("vector", lambda e: e.reciprocal(
                                    out=b_st[:, 0, 11].unsqueeze(1),
                                    in_=b_st[:, 0, 10].unsqueeze(1)),
                                    waits=[("sV", mkn1)], inc="sV")
                                S.do("vector", lambda e, _h=h: e.tensor_scalar(
                                    out=b_y[:, _h * 64:(_h + 1) * 64],
                                    in0=pb[6 + _h % 2][:, 0:64],
                                    scalar1=b_st[:, 0, 11].unsqueeze(1), scalar2=None,
                                    op0=ALU.mult), waits=[("sV", mkn2)], inc="sV")
                                mark[("norm", t, h)] = S.cnt["sV"]

                            for h in range(NH):
                                emit_score(h)
                                if h >= 1:
                                    emit_Y(h - 1)
                            emit_Y(NH - 1)
                            mark[("Ylast",) + key] = mark[("Ymm", t, NH - 1)]

                        for bi, (h0, h1) in enumerate(((0, 7), (7, 14), (14, 16))):
                            wst = [("sV", mk_fk), ("sV", mk_va)]
                            if not prefix:
                                wst.append(("sV", M(("poly", t, NH - 1), 0)))
                            if bi > 0:
                                wst.append(("sV", M(("staddb",) + key + (bi - 1,), 0)))
                            else:
                                wst.append(("sV", Mg("stadd", gi - 1)))
                            for h in range(h0, h1):
                                S.do("tensor", lambda e, _b=b, _h=h, _h0=h0: e.matmul(
                                    out=pb[4][:, (_h - _h0) * 65:(_h - _h0 + 1) * 65],
                                    lhsT=b_fk[:, _b, _h, 0:128], rhs=b_va[:, _b, _h, :],
                                    start=True, stop=True, skip_group_check=True),
                                    waits=(wst if h == h0 else ()))
                                S.do("tensor", lambda e, _b=b, _h=h, _h0=h0: e.matmul(
                                    out=pb[5][0:32, (_h - _h0) * 65:(_h - _h0 + 1) * 65],
                                    lhsT=b_fk[:, _b, _h, 128:160],
                                    rhs=b_va[:, _b, _h, :],
                                    start=True, stop=True, skip_group_check=True),
                                    inc="sP" if h == h1 - 1 else None)
                            mkb = S.cnt["sP"]
                            n = h1 - h0
                            wv = [("sP", mkb)]
                            if gi == 0 and bi == 0:
                                wv.append(("sV", mk_kvinit))
                            S.do("vector", lambda e, _h0=h0, _n=n: e.tensor_tensor(
                                out=KV1[:, _h0:_h0 + _n, :], in0=KV1[:, _h0:_h0 + _n, :],
                                in1=pb[4][:, 0:_n * 65].rearrange("p (h n) -> p h n", h=_n),
                                op=ALU.add), waits=wv)
                            S.do("vector", lambda e, _h0=h0, _n=n: e.tensor_tensor(
                                out=KV2[:, _h0:_h0 + _n, :], in0=KV2[:, _h0:_h0 + _n, :],
                                in1=pb[5][0:32, 0:_n * 65].rearrange("p (h n) -> p h n", h=_n),
                                op=ALU.add), inc="sV")
                            mark[("staddb",) + key + (bi,)] = S.cnt["sV"]
                        mark[("state",) + key] = mkb
                        mark[("stadd",) + key] = S.cnt["sV"]

                        if not prefix:
                            mark[("yT",) + key] = S.do("sync", lambda e: e.dma_start(
                                out=b_yf[:, :, :],
                                in_=b_y[:, :].rearrange("p (B i) -> p B i", B=8),
                                transpose=True),
                                waits=[("sV", mark[("norm", t, NH - 1)]),
                                       ("sP", Mg("wo", gi - 1))], inc="qT")
                            mk_yT = S.cnt["qT"]
                            n2 = 0
                            for c in range(8):
                                for hf in range(2):
                                    n2 += 1
                                    S.do("tensor", lambda e, _c=c, _hf=hf: e.matmul(
                                        out=pb[2 + _hf][:, 0:512], lhsT=b_yf[:, _c, :],
                                        rhs=w_o[:, _c, _hf * 512:(_hf + 1) * 512],
                                        start=(_c == 0), stop=(_c == 7),
                                        skip_group_check=True),
                                        waits=[("qT", mk_yT), ("sV", mk_va)]
                                              if n2 == 1 else (),
                                        inc="sP" if n2 == 16 else None)
                            mark[("wo",) + key] = S.cnt["sP"]
                            for hf in range(2):
                                S.do("vector", lambda e, _b=bx, _hf=hf, _t=t:
                                     e.tensor_tensor(
                                         out=x2[:, _t, _hf * 512:(_hf + 1) * 512],
                                         in0=b_x[:, _b, _hf * 512:(_hf + 1) * 512],
                                         in1=pb[2 + _hf][:, 0:512], op=ALU.add),
                                     waits=[("sP", mark[("wo",) + key])]
                                           if hf == 0 else (),
                                     inc="sV" if hf == 1 else None)
                            mark[("x2add",) + key] = S.cnt["sV"]
                        mark[("xdone",) + key] = S.cnt["sV"]
                    return part2

                emit_front(0)
                for gi in range(NTOT):
                    part2 = emit_body(gi)
                    if gi + 1 < NTOT:
                        emit_front(gi + 1)
                    part2()
                    if gi + 1 >= NP and gi + 1 < NTOT:
                        emit_vproj(gi + 1)

                attn_end = {s: S.cnt[s] for s in S.cnt}

                if debug:
                    for t in range(NT):
                        S.do("scalar", lambda e, _t=t: e.dma_start(
                            out=d_dx2.rearrange("(n p) m -> n p m", p=128)[_t],
                            in_=x2[:, _t, :]),
                            waits=[("sV", mark[("x2add", "m", t)])], inc="qX")

            sbr = lambda name, shape, dt: top.enter_context(
                nc.sbuf_tensor(name, shape, dt, side="right"))
            nf = sbr("nf", [128, 8, TC], BF16)
            nfv = nf[:, :, :].rearrange("p c (n i) -> p c n i", i=128)
            w_g = sbr("w_g", [128, 8, 16, 128], BF16)
            w_u = sbr("w_u", [128, 8, 16, 128], BF16)

            with contextlib.ExitStack() as pf:
                sbp = lambda name, shape, dt: pf.enter_context(nc.sbuf_tensor(name, shape, dt))
                b_l2 = sbp("b_l2", [128, 2, H], F32)
                b_nn = sbp("b_nn", [128, 2, H], F32)
                b_a0 = sbp("b_a0", [128, 2, H], F32)
                b_hi = sbp("b_hi", [128, 2, H], BF16)
                b_lo = sbp("b_lo", [128, 2, H], BF16)
                b_lof = sbp("b_lof", [128, 2, 8, 128], BF16)

                for r in ("scalar", "vector", "tensor", "sync"):
                    for s in ("sP", "sV", "sA", "qT"):
                        S.wait(r, s, attn_end[s])

                for t in range(NT):
                    b = t % 2
                    mkS = emit_ln_stats(lambda hv, _t=t: x2[:, _t, hv * 512:(hv + 1) * 512], b)
                    S.wait("vector", "sA", mkS)
                    mk_l2 = S.do("vector", lambda e, _b=b, _t=t: e.tensor_scalar(
                        out=b_l2[:, _b, :], in0=x2[:, _t, :],
                        scalar1=b_st[:, _b, 4].unsqueeze(1), scalar2=b_st[:, _b, 5].unsqueeze(1),
                        op0=ALU.subtract, op1=ALU.mult),
                        waits=[("sA", M(("s2", t - 2), 0))], inc="sV")
                    mk_a0 = S.do("vector", lambda e, _b=b, _t=t: e.tensor_tensor(
                        out=b_a0[:, _b, :], in0=x2[:, _t, :], in1=b_l2[:, _b, :], op=ALU.add),
                        waits=[("qX", M(("a0w", t - 2), 0)), ("sV", mk_l2)], inc="sV")
                    mark[("a0w", t)] = S.do("scalar", lambda e, _b=b, _t=t: e.dma_start(
                        out=d_out.rearrange("(n p) m -> n p m", p=128)[_t], in_=b_a0[:, _b, :]),
                        waits=[("sV", mk_a0)], inc="qX")
                    S.wait("scalar", "sV", mk_l2)
                    mkS2 = emit_ln_stats(lambda hv, _b=b: b_l2[:, _b, hv * 512:(hv + 1) * 512], b)
                    mark[("s2", t)] = mkS2
                    S.wait("vector", "sA", mkS2)
                    mk_nn = S.do("vector", lambda e, _b=b: e.tensor_scalar(
                        out=b_nn[:, _b, :], in0=b_l2[:, _b, :],
                        scalar1=b_st[:, _b, 4].unsqueeze(1), scalar2=b_st[:, _b, 5].unsqueeze(1),
                        op0=ALU.subtract, op1=ALU.mult), inc="sV")
                    mk_hi = S.do("vector", lambda e, _b=b: e.tensor_copy(
                        out=b_hi[:, _b, :], in_=b_nn[:, _b, :]),
                        waits=[("qT", M(("hiT", t - 2), 0)), ("sV", mk_nn)], inc="sV")
                    mk_hilo = S.do("vector", lambda e, _b=b: e.scalar_tensor_tensor(
                        out=b_lo[:, _b, :], in0=b_hi[:, _b, :], scalar=-1.0,
                        in1=b_nn[:, _b, :], op0=ALU.mult, op1=ALU.add),
                        waits=[("qT", M(("loT", t - 2), 0)), ("sV", mk_hi)], inc="sV")
                    mark[("hiT", t)] = S.do("sync", lambda e, _b=b, _t=t: e.dma_start(
                        out=nfv[:, :, _t, :],
                        in_=b_hi[:, _b, :].rearrange("p (B i) -> p B i", B=8), transpose=True),
                        waits=[("sV", mk_hilo)], inc="qT")
                    mark[("loT", t)] = S.do("sync", lambda e, _b=b: e.dma_start(
                        out=b_lof[:, _b, :, :],
                        in_=b_lo[:, _b, :].rearrange("p (B i) -> p B i", B=8), transpose=True),
                        waits=[("sP", M(("lgmm", t - 2), 0))], inc="qT")
                    mk_loT = S.cnt["qT"]

                    n3 = 0
                    for (lv, rv) in (("hi", rw_h), ("hi", rw_l), ("lo", rw_h)):
                        for c in range(8):
                            n3 += 1
                            if lv == "hi":
                                lf = lambda _c=c, _t=t: nfv[:, _c, _t, :]
                            else:
                                lf = lambda _c=c, _b=b: b_lof[:, _b, _c, :]
                            w = []
                            if n3 == 1:
                                w = [("qT", mk_loT), ("sV", M(("wrt", t - 1), 0))]
                            S.do("tensor", lambda e, _lf=lf, _r=rv, _c=c, _n=n3:
                                 e.matmul(out=pb[1][:, 0:8], lhsT=_lf(), rhs=_r[:, _c, :],
                                          start=(_n == 1), stop=False, skip_group_check=True),
                                 waits=w)
                    S.do("tensor", lambda e: e.matmul(
                        out=pb[1][:, 0:8], lhsT=c_one1[:, :], rhs=brw_h[0:1, :],
                        start=False, stop=False, skip_group_check=True),
                        waits=[("sV", mk_brw)] if t == 0 else ())
                    mark[("lgmm", t)] = S.do("tensor", lambda e: e.matmul(
                        out=pb[1][:, 0:8], lhsT=c_one1[:, :], rhs=brw_l[0:1, :],
                        start=False, stop=True, skip_group_check=True), inc="sP")
                    mk_rm = S.do("vector", lambda e, _t=t: e.reduce_max(
                        out=b_mx[:, _t].unsqueeze(1), in_=pb[1][:, 0:8], axis=AX),
                        waits=[("sP", mark[("lgmm", t)])], inc="sV")
                    mark[("wrt", t)] = S.do("vector", lambda e, _t=t: e.tensor_tensor(
                        out=b_eq[:, _t, :], in0=pb[1][:, 0:8],
                        in1=b_mx[:, _t].unsqueeze(1).broadcast_to([128, 8]), op=ALU.is_equal),
                        waits=[("sV", mk_rm)], inc="sV")

                S.wait("scalar", "sV", mark[("wrt", NT - 1)])
                mk_sg = S.do("scalar", lambda e: e.activation(
                    out=b_mx[:, :], in_=b_mx[:, :], func=AF.Sigmoid), inc="sA")
                for t in range(NT):
                    S.do("vector", lambda e, _t=t: e.tensor_scalar(
                        out=w_rt[:, 0:8, _t], in0=b_eq[:, _t, :],
                        scalar1=b_mx[:, _t].unsqueeze(1), scalar2=None, op0=ALU.mult),
                        waits=[("sA", mk_sg)] if t == 0 else (),
                        inc="sV" if t == NT - 1 else None)
                mk_wrtall = S.cnt["sV"]

                S.wait("vector", "qT", mark[("hiT", NT - 1)])
                S.wait("vector", "sP", mark[("lgmm", NT - 1)])
                for c in range(8):
                    S.do("vector", lambda e, _c=c: e.tensor_scalar(
                        out=nf[:, _c, :], in0=nf[:, _c, :],
                        scalar1=lnwb[:, 0, _c].unsqueeze(1), scalar2=lnwb[:, 1, _c].unsqueeze(1),
                        op0=ALU.mult, op1=ALU.add),
                        inc="sV" if c == 7 else None)
                mk_nf = S.cnt["sV"]
                preffn_end = {s: S.cnt[s] for s in S.cnt}

                if debug:
                    for c in range(8):
                        S.do("gpsimd", lambda e, _c=c: e.dma_start(
                            out=d_dnf.rearrange("(c p) n -> c p n", p=128)[_c],
                            in_=nf[:, _c, :]), waits=[("sV", mk_nf)], inc="qW")
                    S.do("scalar", lambda e: e.dma_start(
                        out=d_dwr, in_=w_rt[:, :, :].rearrange("p e t -> p (e t)")),
                        waits=[("sV", mk_wrtall)], inc="qX")


        if phases == "full":
            with contextlib.ExitStack() as ff:
                sbf = lambda name, shape, dt: ff.enter_context(nc.sbuf_tensor(name, shape, dt))
                w_d = sbf("w_d", [128, 16, 2, 512], BF16)
                hm_tm = sbf("hm_tm", [128, 4, IO], BF16)
                hm_io = sbf("hm_io", [128, 16, 512], BF16)
                b_sv = sbf("b_sv", [128, 2, 512], F32)
                dtemp = sbf("dtemp", [128, 2, 4, H], F32)

                S.wait("gpsimd", "sP", attn_end["sP"])
                S.wait("gpsimd", "sV", attn_end["sV"])
                S.wait("gpsimd", "sA", attn_end["sA"])
                S.wait("gpsimd", "qT", attn_end["qT"])
                for r in ("tensor", "vector", "scalar", "sync"):
                    S.wait(r, "sV", mk_nf)
                    S.wait(r, "sV", mk_wrtall)
                    S.wait(r, "qX", preffn_end["qX"])
                    S.wait(r, "qT", preffn_end["qT"])

                def lin(e_i, g, tl):
                    return (e_i * NG + g) * 4 + tl

                def emit_gu_phase(e_i, g):
                    base_step = (e_i * NG + g) * 16
                    j_outer = (g == NG - 1)
                    order = ([(tl, j) for j in range(4) for tl in range(4)] if j_outer
                             else [(tl, j) for tl in range(4) for j in range(4)])
                    for si, (tl, j) in enumerate(order):
                        st = base_step + si
                        T = 4 * g + tl
                        L = lin(e_i, g, tl)
                        sl = st % 2
                        wgu = [("qW", mark[("gld", e_i, j)]),
                               ("qW", mark[("uld", e_i, j)]),
                               ("sV", M(("hmS", st - 2), 0))]
                        for c in range(8):
                            S.do("tensor", lambda e, _c=c, _T=T, _j=j, _sl=sl:
                                 e.matmul(
                                     out=pb[_sl][:, 0:512],
                                     lhsT=nfv[:, _c, _T, :],
                                     rhs=w_g[:, _c, 4 * _j:4 * _j + 4, :].rearrange(
                                         "p j n -> p (j n)"),
                                     start=(_c == 0), stop=(_c == 7),
                                     skip_group_check=True),
                                 waits=wgu if c == 0 else ())
                            S.do("tensor", lambda e, _c=c, _T=T, _j=j, _sl=sl:
                                 e.matmul(
                                     out=pb[2 + _sl][:, 0:512],
                                     lhsT=nfv[:, _c, _T, :],
                                     rhs=w_u[:, _c, 4 * _j:4 * _j + 4, :].rearrange(
                                         "p j n -> p (j n)"),
                                     start=(_c == 0), stop=(_c == 7),
                                     skip_group_check=True),
                                 inc="sP" if c == 7 else None)
                        mark[("gu", L, j)] = S.cnt["sP"]
                        mark[("guq", e_i, j)] = S.cnt["sP"]
                        mark[("sv", L, j)] = S.do("scalar",
                            lambda e, _e=e_i, _T=T, _sl=sl: e.activation(
                                out=b_sv[:, _sl, :], in_=pb[_sl][:, 0:512],
                                func=AF.Silu, scale=w_rt[:, _e, _T].unsqueeze(1)),
                            waits=[("sP", mark[("gu", L, j)]),
                                   ("sV", M(("hmS", st - 2), 0))], inc="sA")
                        m = S.do("vector",
                            lambda e, _e=e_i, _T=T, _j=j, _sl=sl, _tl=tl:
                            e.scalar_tensor_tensor(
                                out=hm_tm[:, _tl, 512 * _j:512 * (_j + 1)],
                                in0=pb[2 + _sl][:, 0:512],
                                scalar=w_rt[:, _e, _T].unsqueeze(1),
                                in1=b_sv[:, _sl, :], op0=ALU.mult, op1=ALU.mult),
                            waits=[("sA", mark[("sv", L, j)]),
                                   ("qT", M(("hmT", L - 4), 0))], inc="sV")
                        mark[("hmS", st)] = m
                        mark[("hm", L, j)] = m
                    if g == NG - 1:
                        mark[("guend", e_i)] = S.cnt["sP"]

                def emit_hmT(e_i, g):
                    eg = e_i * NG + g
                    for tl in range(4):
                        L = lin(e_i, g, tl)
                        dprev = M(("dnmm", ((eg - 1) * 4 + tl) * 2 + 1), 0)
                        mark[("hmT", L)] = S.do("sync", lambda e, _sl=tl, _tl=tl:
                            e.dma_start(
                                out=hm_io[:, :, :].rearrange(
                                    "p j (n i) -> p j n i", i=128)[:, :, _tl, :],
                                in_=hm_tm[:, _sl, :].rearrange("p (B i) -> p B i", B=16),
                                transpose=True),
                            waits=[("sV", mark[("hm", L, 3)]), ("sP", dprev)],
                            inc="qT")

                def emit_down(e_i, g):
                    eg = e_i * NG + g
                    for tl in range(4):
                        L = lin(e_i, g, tl)
                        for hf in range(2):
                            bank = 4 + (2 * tl + hf) % 4
                            D = (eg * 4 + tl) * 2 + hf
                            wdn = [("qT", mark[("hmT", L)]),
                                   ("qW", mark[("dld", e_i, hf)]),
                                   ("sV", M(("dnev", D - 4), 0))]
                            for j in range(16):
                                S.do("tensor", lambda e, _j=j, _tl=tl, _hf=hf, _bk=bank:
                                     e.matmul(
                                         out=pb[_bk][:, 0:512],
                                         lhsT=hm_io[:, _j, :].rearrange(
                                             "p (n i) -> p n i", i=128)[:, _tl, :],
                                         rhs=w_d[:, _j, _hf, :],
                                         start=(_j == 0), stop=(_j == 15),
                                         skip_group_check=True),
                                     waits=wdn if j == 0 else (),
                                     inc="sP" if j == 15 else None)
                            mark[("dnmm", D)] = S.cnt["sP"]
                            mark[("dnev", D)] = S.do("vector",
                                lambda e, _tl=tl, _hf=hf, _bk=bank, _g=g: e.tensor_copy(
                                    out=dtemp[:, _g % 2, _tl, _hf * 512:(_hf + 1) * 512],
                                    in_=pb[_bk][:, 0:512]),
                                waits=[("sP", mark[("dnmm", D)]),
                                       ("qW", M(("accq", eg - 2), 0))],
                                inc="sV")
                    Dlast = (eg * 4 + 3) * 2 + 1
                    mark[("dn", e_i, g)] = mark[("dnmm", Dlast)]
                    if g == NG - 1:
                        mark[("dend", e_i)] = mark[("dn", e_i, g)]
                    mark[("accq", eg)] = S.do("gpsimd",
                        lambda e, _g=g: e.dma_start(
                            out=d_out.rearrange("(G T p) h -> G p T h", p=128, T=4)[_g],
                            in_=dtemp[:, _g % 2, :, :], accum_op=ALU.add),
                        waits=[("sV", mark[("dnev", Dlast)]),
                               ("qX", mark[("a0w", 4 * g + 3)])], inc="qW")

                def emit_dld(e_i, first):
                    dld_w = [("sP", M(("dend", e_i - 1), 0))]
                    if first:
                        dld_w += [("sV", preffn_end["sV"]), ("qT", preffn_end["qT"]),
                                  ("sP", preffn_end["sP"]), ("sA", preffn_end["sA"]),
                                  ("qX", preffn_end["qX"])]
                    for s_ in range(2):
                        mark[("dld", e_i, s_)] = S.do("gpsimd", lambda e, _e=e_i, _s=s_:
                            e.dma_start(
                                out=w_d[:, :, _s, :],
                                in_=d_dw[_e][:, 512 * _s:512 * (_s + 1)].rearrange(
                                    "(j p) n -> p j n", p=128)),
                            waits=dld_w if s_ == 0 else (), inc="qW")

                pending = None
                for e_i in range(NE):
                    for q in range(4):
                        mark[("gld", e_i, q)] = S.do("gpsimd", lambda e, _e=e_i, _q=q:
                            e.dma_start(
                                out=w_g[:, :, 4 * _q:4 * _q + 4, :],
                                in_=d_gw[_e][:, 512 * _q:512 * (_q + 1)].rearrange(
                                    "(c p) (j n) -> p c j n", p=128, n=128)),
                            waits=[("sP", M(("guq", e_i - 1, q), 0))], inc="qW")
                        mark[("uld", e_i, q)] = S.do("gpsimd", lambda e, _e=e_i, _q=q:
                            e.dma_start(
                                out=w_u[:, :, 4 * _q:4 * _q + 4, :],
                                in_=d_uw[_e][:, 512 * _q:512 * (_q + 1)].rearrange(
                                    "(c p) (j n) -> p c j n", p=128, n=128)),
                            inc="qW")
                    if e_i == 0:
                        emit_dld(0, first=True)
                    for g in range(NG):
                        emit_gu_phase(e_i, g)
                        if pending is not None:
                            emit_down(*pending)
                            if pending == (e_i - 1, NG - 1):
                                emit_dld(e_i, first=False)
                        emit_hmT(e_i, g)
                        pending = (e_i, g)
                emit_down(*pending)

                S.wait("gpsimd", "qW", S.cnt["qW"])
                S.wait("gpsimd", "qX", S.cnt["qX"])
        else:
            S.wait("gpsimd", "qX", S.cnt["qX"])
            S.wait("gpsimd", "sV", S.cnt["sV"])

        with (
            nc.semaphore("qX") as h_qx,
            nc.semaphore("qT") as h_qt,
            nc.semaphore("qW") as h_qw,
            nc.semaphore("sP") as h_sp,
            nc.semaphore("sV") as h_sv,
            nc.semaphore("sA") as h_sa,
            nc.Block() as block,
        ):
            S.bind({"qX": h_qx, "qT": h_qt, "qW": h_qw,
                    "sP": h_sp, "sV": h_sv, "sA": h_sa})

            @block.sync
            def _(eng):
                for g, fn in S.ops["sync"]:
                    if _TRUNC is None or g < _TRUNC:
                        fn(eng)

            @block.scalar
            def _(eng):
                for g, fn in S.ops["scalar"]:
                    if _TRUNC is None or g < _TRUNC:
                        fn(eng)

            @block.gpsimd
            def _(eng):
                for g, fn in S.ops["gpsimd"]:
                    if _TRUNC is None or g < _TRUNC:
                        fn(eng)

            @block.tensor
            def _(eng):
                for g, fn in S.ops["tensor"]:
                    if _TRUNC is None or g < _TRUNC:
                        fn(eng)

            @block.vector
            def _(eng):
                for g, fn in S.ops["vector"]:
                    if _TRUNC is None or g < _TRUNC:
                        fn(eng)

    return nc


def _prep_inputs(inputs):
    x = np.asarray(inputs["L0_attn_hidden_states"], np.float32)
    Wq = np.asarray(inputs["L0_attn_Wq"], np.float32)
    Wk = np.asarray(inputs["L0_attn_Wk"], np.float32)
    Wv = np.asarray(inputs["L0_attn_Wv"], np.float32)
    Wo = np.asarray(inputs["L0_attn_Wo"], np.float32)
    lnw = np.asarray(inputs["L0_ffn_ln_w"], np.float32)
    lnb = np.asarray(inputs["L0_ffn_ln_b"], np.float32)
    rw = np.asarray(inputs["L0_ffn_router_w"], np.float32)
    gw = np.asarray(inputs["L0_ffn_gate_w"], np.float32)
    uw = np.asarray(inputs["L0_ffn_up_w"], np.float32)
    dw = np.asarray(inputs["L0_ffn_down_w"], np.float32)
    shg = np.asarray(inputs["L0_ffn_sh_gate"], np.float32)
    shu = np.asarray(inputs["L0_ffn_sh_up"], np.float32)
    shd = np.asarray(inputs["L0_ffn_sh_down"], np.float32)

    wqT = np.ascontiguousarray(Wq.T)
    wkT_pad = np.zeros((H, 512), np.float32)
    for h in range(NH):
        wkT_pad[:, 32 * h:32 * h + 16] = Wk.T[:, 16 * h:16 * h + 16]
    wvT = np.ascontiguousarray(Wv.T)
    woT = np.ascontiguousarray(Wo.T)
    rwm = np.ascontiguousarray(rw.reshape(8, 128, 8).transpose(1, 0, 2).reshape(128, 64))
    lnw_m = np.ascontiguousarray(lnw.reshape(8, 128).T)
    lnb_m = np.ascontiguousarray(lnb.reshape(8, 128).T)
    cm = np.ascontiguousarray(
        (np.arange(128)[:, None] <= np.arange(128)[None, :]).astype(np.float32))
    gw9 = np.ascontiguousarray(np.concatenate([gw, shg[None]], axis=0))
    uw9 = np.ascontiguousarray(np.concatenate([uw, shu[None]], axis=0))
    dw9 = np.ascontiguousarray(np.concatenate([dw, shd[None]], axis=0))

    in_maps = []
    for c in range(NCORE):
        batch, chunk = divmod(c, 4)
        t0 = chunk * TC
        xm = np.ascontiguousarray(x[batch, t0:t0 + TC, :])
        xp = np.zeros((NP * 128, H), np.float32)
        npre = min(t0, NP * 128)
        xp[:npre] = x[batch, :npre, :]
        valid = (np.arange(NP * 128) < npre)
        pm = np.ascontiguousarray(valid.reshape(NP, 128).T.astype(np.float32))
        in_maps.append({
            "x_main": xm, "x_pre": xp,
            "pmask_h": np.ascontiguousarray(0.5 * pm), "pmask_1": pm,
            "cmask": cm, "wqT": wqT, "wkT": wkT_pad, "wvT": wvT, "woT": woT,
            "rwm": rwm, "lnw": lnw_m, "lnb": lnb_m,
            "gw": gw9, "uw": uw9, "dw": dw9,
        })
    return in_maps


_cache = {}


def kernel(**inputs):
    key = (_PHASES, _DEBUG)
    if key not in _cache:
        _cache[key] = build_program(debug=_DEBUG, phases=_PHASES)
    nc = _cache[key]
    in_maps = _prep_inputs(inputs)
    res = run_bass_kernel_spmd(nc, in_maps, core_ids=list(range(NCORE)))
    outs = [np.asarray(res.results[c]["out"]) for c in range(NCORE)]
    full = np.stack([np.concatenate(outs[0:4], axis=0),
                     np.concatenate(outs[4:8], axis=0)], axis=0)
    if _DEBUG:
        kernel.debug = res.results
    return full


# version 06
# speedup vs baseline: 1.0209x; 1.0209x over previous
import contextlib
import numpy as np
import concourse.bass as bass
import concourse.mybir as mybir
from concourse.bass_utils import run_bass_kernel_spmd

F32 = mybir.dt.float32
BF16 = mybir.dt.bfloat16
AF = mybir.ActivationFunctionType
ALU = mybir.AluOpType
AX = mybir.AxisListType.X

H = 1024
NH = 16
TC = 2048
NT = 16
NP = 48
NE = 9
IO = 2048
NG = 4
NCORE = 8
EPS_LN = 1e-5
EPS_AT = 1e-12
RS2 = float(1.0 / np.sqrt(2.0))
SQ_SCALE = float(1.0 / (2.0 * np.sqrt(2.0)))

_PHASES = "full"
_DEBUG = False
_TRUNC = None
_LAST_S = None

OFF_PROD = 16
OFF_DIAG = 136
COL_ONES = 152
DPHI = 160


class Sched:
    ROLES = ("sync", "scalar", "gpsimd", "tensor", "vector")

    def __init__(self):
        self.ops = {r: [] for r in self.ROLES}
        self.cnt = {s: 0 for s in ("qX", "qT", "qW", "sP", "sV", "sA")}
        self.gidx = 0

    def do(self, role, fn, waits=(), inc=None):
        inc_by = None
        if inc is not None:
            inc_by = 16 if inc.startswith("q") else 1
            self.cnt[inc] += inc_by
        waits = tuple((s, v) for (s, v) in waits if v > 0)

        def run(eng, _fn=fn, _waits=waits, _inc=inc, _by=inc_by):
            for (s, v) in _waits:
                eng.wait_ge(self._h[s], v)
            ins = _fn(eng)
            if _inc is not None:
                ins.then_inc(self._h[_inc], _by)
        self.ops[role].append((self.gidx, run))
        self.gidx += 1
        return self.cnt[inc] if inc is not None else None

    def wait(self, role, sem, val):
        if val <= 0:
            return

        def run(eng, _s=sem, _v=val):
            eng.wait_ge(self._h[_s], _v)
        self.ops[role].append((self.gidx, run))
        self.gidx += 1

    def bind(self, handles):
        self._h = handles


def build_program(debug=False, phases="full"):
    global _LAST_S
    nc = bass.Bass("TRN2", target_bir_lowering=False)
    S = Sched()
    _LAST_S = S
    mark = {}
    M = mark.get

    d_x = nc.dram_tensor("x_main", [TC, H], F32, kind="ExternalInput").ap()
    d_xp = nc.dram_tensor("x_pre", [NP * 128, H], F32, kind="ExternalInput").ap()
    d_pmh = nc.dram_tensor("pmask_h", [128, NP], F32, kind="ExternalInput").ap()
    d_pm1 = nc.dram_tensor("pmask_1", [128, NP], F32, kind="ExternalInput").ap()
    d_cm = nc.dram_tensor("cmask", [128, 128], F32, kind="ExternalInput").ap()
    d_wq = nc.dram_tensor("wqT", [H, 256], F32, kind="ExternalInput").ap()
    d_wk = nc.dram_tensor("wkT", [H, 512], F32, kind="ExternalInput").ap()
    d_wv = nc.dram_tensor("wvT", [H, H], F32, kind="ExternalInput").ap()
    d_wo = nc.dram_tensor("woT", [H, H], F32, kind="ExternalInput").ap()
    d_rw = nc.dram_tensor("rwm", [128, 64], F32, kind="ExternalInput").ap()
    d_lnw = nc.dram_tensor("lnw", [128, 8], F32, kind="ExternalInput").ap()
    d_lnb = nc.dram_tensor("lnb", [128, 8], F32, kind="ExternalInput").ap()
    d_gw = nc.dram_tensor("gw", [NE, H, IO], F32, kind="ExternalInput").ap()
    d_uw = nc.dram_tensor("uw", [NE, H, IO], F32, kind="ExternalInput").ap()
    d_dw = nc.dram_tensor("dw", [NE, IO, H], F32, kind="ExternalInput").ap()
    d_out = nc.dram_tensor("out", [TC, H], F32, kind="ExternalOutput").ap()
    if debug:
        d_dx2 = nc.dram_tensor("dbg_x2", [TC, H], F32, kind="ExternalOutput").ap()
        d_dnf = nc.dram_tensor("dbg_nf", [H, TC], F32, kind="ExternalOutput").ap()
        d_dwr = nc.dram_tensor("dbg_wrt", [128, NE * NT], F32, kind="ExternalOutput").ap()

    with contextlib.ExitStack() as top:
        sb0 = lambda name, shape, dt: top.enter_context(nc.sbuf_tensor(name, shape, dt))
        c_pmh = sb0("c_pmh", [128, NP], F32)
        c_pm1 = sb0("c_pm1", [128, NP], F32)
        c_cm = sb0("c_cm", [128, 128], BF16)
        c_one1 = sb0("c_one1", [1, 128], BF16)
        b_st = sb0("b_st", [128, 2, 16], F32)
        b_sq = sb0("b_sq", [128, 512], F32)
        w_rt = sb0("w_rt", [128, NE, NT], F32)
        b_mx = sb0("b_mx", [128, NT], F32)
        b_eq = sb0("b_eq", [128, NT, 8], F32)
        rw_f = sb0("rw_f", [128, 8, 8], F32)
        rw_h = sb0("rw_h", [128, 8, 8], BF16)
        rw_l = sb0("rw_l", [128, 8, 8], BF16)
        rw_raw = sb0("rw_raw", [128, 8, 8], F32)
        rwr_h = sb0("rwr_h", [128, 8, 8], BF16)
        rwr_l = sb0("rwr_l", [128, 8, 8], BF16)
        lnwb = sb0("lnwb", [128, 2, 8], F32)
        lnb_h = sb0("lnb_h", [128, 8], BF16)
        lnb_l = sb0("lnb_l", [128, 8], BF16)
        c_pc = sb0("c_pc", [128, 2], F32)
        brw_f = sb0("brw_f", [1, 8], F32)
        brw_h = sb0("brw_h", [1, 8], BF16)
        brw_l = sb0("brw_l", [1, 8], BF16)
        pb = [top.enter_context(nc.psum_tensor(f"pb{i}", [128, 512], F32)) for i in range(8)]

        S.do("scalar", lambda e: e.dma_start(out=c_pmh[:, :], in_=d_pmh), inc="qX")
        S.do("scalar", lambda e: e.dma_start(out=c_pm1[:, :], in_=d_pm1), inc="qX")
        S.do("gpsimd", lambda e: e.dma_start(out=c_cm[:, :], in_=d_cm), inc="qW")
        S.do("scalar", lambda e: e.dma_start(out=lnwb[:, 0, :], in_=d_lnw), inc="qX")
        S.do("scalar", lambda e: e.dma_start(out=lnwb[:, 1, :], in_=d_lnb), inc="qX")
        S.do("scalar", lambda e: e.dma_start(
            out=rw_f[:, :, :].rearrange("p c e -> p (c e)"), in_=d_rw), inc="qX")
        mk_rwraw = S.do("scalar", lambda e: e.dma_start(
            out=rw_raw[:, :, :].rearrange("p c e -> p (c e)"), in_=d_rw), inc="qX")
        mk_consts = S.cnt["qX"]

        S.do("vector", lambda e: e.memset(c_one1[:, :], 1.0))
        S.do("vector", lambda e: e.memset(c_pc[:, 0:1], SQ_SCALE))
        S.do("vector", lambda e: e.memset(c_pc[:, 1:2], RS2))
        S.wait("vector", "qX", mk_consts)
        mkrwf = None
        for c in range(8):
            mkrwf = S.do("vector", lambda e, _c=c: e.tensor_scalar(
                out=rw_f[:, _c, :], in0=rw_f[:, _c, :],
                scalar1=lnwb[:, 0, _c].unsqueeze(1), scalar2=None, op0=ALU.mult),
                inc="sV" if c == 7 else None)
        S.do("vector", lambda e: e.tensor_copy(out=lnb_h[:, :], in_=lnwb[:, 1, :]))
        mkh1 = S.do("vector", lambda e: e.tensor_copy(
            out=rw_h[:, :, :], in_=rw_f[:, :, :]), waits=[("sV", mkrwf)], inc="sV")
        S.do("vector", lambda e: e.tensor_copy(out=rwr_h[:, :, :], in_=rw_raw[:, :, :]),
             waits=[("sV", mkh1)])
        S.do("vector", lambda e: e.scalar_tensor_tensor(
            out=rw_l[:, :, :], in0=rw_h[:, :, :], scalar=-1.0, in1=rw_f[:, :, :],
            op0=ALU.mult, op1=ALU.add))
        S.do("vector", lambda e: e.scalar_tensor_tensor(
            out=lnb_l[:, :], in0=lnb_h[:, :], scalar=-1.0, in1=lnwb[:, 1, :],
            op0=ALU.mult, op1=ALU.add))
        mk_split = S.do("vector", lambda e: e.scalar_tensor_tensor(
            out=rwr_l[:, :, :], in0=rwr_h[:, :, :], scalar=-1.0, in1=rw_raw[:, :, :],
            op0=ALU.mult, op1=ALU.add), inc="sV")
        S.do("vector", lambda e: e.memset(w_rt[:, 8, :], 1.0))
        S.wait("tensor", "sV", mk_split)
        nmm = [0]
        for (lh, rh) in ((lnb_h, rwr_h), (lnb_h, rwr_l), (lnb_l, rwr_h)):
            for c in range(8):
                nmm[0] += 1
                S.do("tensor", lambda e, _l=lh, _r=rh, _c=c, _n=nmm[0]:
                     e.matmul(out=pb[1][0:1, 0:8], lhsT=_l[:, _c].unsqueeze(1),
                              rhs=_r[:, _c, :], start=(_n == 1), stop=(_n == 24),
                              skip_group_check=True),
                     inc="sP" if nmm[0] == 24 else None)
        mk_brwmm = S.cnt["sP"]
        S.wait("vector", "sP", mk_brwmm)
        mkb1 = S.do("vector", lambda e: e.tensor_copy(
            out=brw_f[0:1, :], in_=pb[1][0:1, 0:8]), inc="sV")
        mkb2 = S.do("vector", lambda e: e.tensor_copy(
            out=brw_h[0:1, :], in_=brw_f[0:1, :]), waits=[("sV", mkb1)], inc="sV")
        mk_brw = S.do("vector", lambda e: e.scalar_tensor_tensor(
            out=brw_l[0:1, :], in0=brw_h[0:1, :], scalar=-1.0, in1=brw_f[0:1, :],
            op0=ALU.mult, op1=ALU.add), waits=[("sV", mkb2)], inc="sV")

        with contextlib.ExitStack() as mid:
            x2 = mid.enter_context(nc.sbuf_tensor("x2", [128, NT, H], F32))

            with contextlib.ExitStack() as att:
                sba = lambda name, shape, dt: att.enter_context(nc.sbuf_tensor(name, shape, dt))
                w_q = sba("w_q", [128, 8, 256], BF16)
                w_k = sba("w_k", [128, 8, 512], BF16)
                w_v = sba("w_v", [128, 8, H], BF16)
                w_o = sba("w_o", [128, 8, H], BF16)
                b_x = sba("b_x", [128, 3, H], F32)
                b_ln1 = sba("b_ln1", [128, 2, H], BF16)
                b_lf = sba("b_lf", [128, 2, 8, 128], BF16)
                b_qh = sba("b_qh", [128, 2, 256], F32)
                b_kh = sba("b_kh", [128, 2, 512], F32)
                b_kb = sba("b_kb", [128, 2048], BF16)
                b_kf = sba("b_kf", [128, 2, 16, 128], BF16)
                b_va = sba("b_va", [128, 2, NH, 65], BF16)
                b_fq = sba("b_fq", [128, NH, 256], BF16)
                b_PH = sba("b_PH", [128, 2, 32, 128], BF16)
                b_fk = sba("b_fk", [128, 2, NH, DPHI], BF16)
                b_A = sba("b_A", [128, 2, 128], BF16)
                b_ps = sba("b_ps", [128, 2, 128], F32)
                b_y = sba("b_y", [128, H], BF16)
                b_yf = sba("b_yf", [128, 8, 128], BF16)
                KV1 = sba("KV1", [128, NH, 65], F32)
                KV2 = sba("KV2", [32, NH, 65], F32)
                KV1b = sba("KV1b", [128, NH, 65], BF16)
                KV2b = sba("KV2b", [32, NH, 65], BF16)

                for (dst, src) in ((w_q, d_wq), (w_k, d_wk), (w_v, d_wv), (w_o, d_wo)):
                    S.do("gpsimd", lambda e, _d=dst, _s=src: e.dma_start(
                        out=_d[:, :, :], in_=_s.rearrange("(c p) n -> p c n", p=128)),
                        inc="qW")
                mk_wattn = S.cnt["qW"]

                S.do("vector", lambda e: e.memset(b_fq[:, :, COL_ONES:256], 0.0))
                S.do("vector", lambda e: e.memset(b_fq[:, :, COL_ONES], 1.0))
                S.do("vector", lambda e: e.memset(b_fk[:, :, :, 153:DPHI], 0.0))
                S.do("vector", lambda e: e.memset(KV1[:, :, :], 0.0))
                mk_kvinit = S.do("vector", lambda e: e.memset(KV2[:, :, :], 0.0), inc="sV")

                def emit_ln_stats(src_fn, par):
                    for hv in range(2):
                        S.do("scalar", lambda e, _h=hv, _p=par: e.activation(
                            out=b_sq[:, :], in_=src_fn(_h), func=AF.Copy,
                            accum_out=b_st[:, _p, 0 + _h].unsqueeze(1)))
                    mkA = None
                    for hv in range(2):
                        mkA = S.do("scalar", lambda e, _h=hv, _p=par: e.activation(
                            out=b_sq[:, :], in_=src_fn(_h), func=AF.Square,
                            accum_out=b_st[:, _p, 2 + _h].unsqueeze(1)),
                            inc="sA" if hv == 1 else None)
                    S.wait("vector", "sA", mkA)
                    S.do("vector", lambda e, _p=par: e.tensor_scalar(
                        out=b_st[:, _p, 4].unsqueeze(1), in0=b_st[:, _p, 0].unsqueeze(1),
                        scalar1=b_st[:, _p, 1].unsqueeze(1), scalar2=1.0 / H,
                        op0=ALU.add, op1=ALU.mult))
                    mk1 = S.do("vector", lambda e, _p=par: e.tensor_scalar(
                        out=b_st[:, _p, 7].unsqueeze(1), in0=b_st[:, _p, 2].unsqueeze(1),
                        scalar1=b_st[:, _p, 3].unsqueeze(1), scalar2=1.0 / H,
                        op0=ALU.add, op1=ALU.mult), inc="sV")
                    mk2 = S.do("vector", lambda e, _p=par: e.tensor_scalar(
                        out=b_st[:, _p, 6].unsqueeze(1), in0=b_st[:, _p, 4].unsqueeze(1),
                        scalar1=b_st[:, _p, 4].unsqueeze(1), scalar2=None, op0=ALU.mult),
                        waits=[("sV", mk1)], inc="sV")
                    mk3 = S.do("vector", lambda e, _p=par: e.tensor_scalar(
                        out=b_st[:, _p, 8].unsqueeze(1), in0=b_st[:, _p, 7].unsqueeze(1),
                        scalar1=b_st[:, _p, 6].unsqueeze(1), scalar2=EPS_LN,
                        op0=ALU.subtract, op1=ALU.add), waits=[("sV", mk2)], inc="sV")
                    mkV = S.do("vector", lambda e, _p=par: e.reciprocal(
                        out=b_st[:, _p, 9].unsqueeze(1), in_=b_st[:, _p, 8].unsqueeze(1)),
                        waits=[("sV", mk3)], inc="sV")
                    S.wait("scalar", "sV", mkV)
                    return S.do("scalar", lambda e, _p=par: e.activation(
                        out=b_st[:, _p, 5].unsqueeze(1), in_=b_st[:, _p, 9].unsqueeze(1),
                        func=AF.Sqrt), inc="sA")

                NTOT = NP + NT

                def keyof(gi):
                    return ("p", gi) if gi < NP else ("m", gi - NP)

                def Mg(name, gi):
                    if gi < 0:
                        return 0
                    return M((name,) + keyof(gi), 0)

                def phi_ops(src_fn, dst_fn, waits0, inc=None):
                    for j in range(1, 16):
                        off = OFF_PROD + j * (j - 1) // 2
                        sv = src_fn(j, 1)
                        S.do("vector", lambda e, _j=j, _o=off, _sv=sv: e.tensor_tensor(
                            out=dst_fn(_o, _j), in0=src_fn(0, _j),
                            in1=_sv.broadcast_to(tuple(_sv.shape[0:2]) + (_j,)),
                            op=ALU.mult), waits=(waits0 if j == 1 else ()))
                    S.do("vector", lambda e: e.scalar_tensor_tensor(
                        out=dst_fn(OFF_DIAG, 16), in0=src_fn(0, 16), scalar=RS2,
                        in1=src_fn(0, 16), op0=ALU.mult, op1=ALU.mult))
                    return S.do("vector", lambda e: e.tensor_copy(
                        out=dst_fn(0, 16), in_=src_fn(0, 16)), inc=inc)

                def emit_frontA(gi):
                    prefix = gi < NP
                    t = gi if prefix else gi - NP
                    b = gi % 2
                    bx = gi % 3
                    x_src = (d_xp if prefix else d_x).rearrange("(n p) m -> n p m", p=128)[t]
                    S.do("scalar", lambda e, _s=x_src, _b=bx: e.dma_start(
                        out=b_x[:, _b, :], in_=_s),
                        waits=[("sV", Mg("xdone", gi - 3))], inc="qX")
                    mk_xld = S.cnt["qX"]
                    S.wait("scalar", "qX", mk_xld)
                    mkS = emit_ln_stats(
                        lambda hv, _b=bx: b_x[:, _b, hv * 512:(hv + 1) * 512], b)
                    S.wait("vector", "sA", mkS)
                    mark[("ln1",) + keyof(gi)] = S.do(
                        "vector", lambda e, _b=b, _bx=bx: e.tensor_scalar(
                            out=b_ln1[:, _b, :], in0=b_x[:, _bx, :],
                            scalar1=b_st[:, _b, 4].unsqueeze(1),
                            scalar2=b_st[:, _b, 5].unsqueeze(1),
                            op0=ALU.subtract, op1=ALU.mult),
                        waits=[("qT", Mg("ln1T", gi - 2))], inc="sV")

                def emit_frontT(gi):
                    b = gi % 2
                    mark[("ln1T",) + keyof(gi)] = S.do("sync", lambda e, _b=b: e.dma_start(
                        out=b_lf[:, _b, :, :],
                        in_=b_ln1[:, _b, :].rearrange("p (B i) -> p B i", B=8),
                        transpose=True),
                        waits=[("sV", mark[("ln1",) + keyof(gi)]),
                               ("sP", Mg("projv", gi - 2))], inc="qT")

                def emit_frontP(gi):
                    prefix = gi < NP
                    b = gi % 2
                    key = keyof(gi)
                    pe_w = [("qT", mark[("ln1T",) + key]), ("qW", mk_wattn),
                            ("sV", mk_brw),
                            ("sA", Mg("khdone", gi - 1)),
                            ("sV", Mg("ktm", gi - 1))]
                    if prefix:
                        pe_w.append(("sV", Mg("vadone", gi - 1)))
                    mms = []
                    for c in range(8):
                        if not prefix:
                            mms.append((c, pb[1], 256, w_q, 0))
                        mms.append((c, pb[0], 512, w_k, 0))
                        if prefix:
                            mms.append((c, pb[2], 512, w_v, 0))
                            mms.append((c, pb[3], 512, w_v, 512))
                    for i, (c, bank, n, ww, c0) in enumerate(mms):
                        S.do("tensor", lambda e, _c=c, _b=b, _bk=bank, _n=n, _w=ww, _c0=c0:
                             e.matmul(out=_bk[:, 0:_n], lhsT=b_lf[:, _b, _c, :],
                                      rhs=_w[:, _c, _c0:_c0 + _n],
                                      start=(_c == 0), stop=(_c == 7),
                                      skip_group_check=True),
                             waits=pe_w if i == 0 else (),
                             inc="sP" if i == len(mms) - 1 else None)
                    mark[("projqk",) + key] = S.cnt["sP"]
                    if prefix:
                        mark[("projv",) + key] = S.cnt["sP"]

                def emit_vproj(gi):
                    b = gi % 2
                    key = keyof(gi)
                    pe_w = [("sV", Mg("x2add", gi - 1)), ("sV", Mg("vadone", gi - 1))]
                    n3 = 0
                    for c in range(8):
                        for hf in range(2):
                            n3 += 1
                            S.do("tensor", lambda e, _c=c, _b=b, _hf=hf: e.matmul(
                                out=pb[2 + _hf][:, 0:512], lhsT=b_lf[:, _b, _c, :],
                                rhs=w_v[:, _c, _hf * 512:(_hf + 1) * 512],
                                start=(_c == 0), stop=(_c == 7), skip_group_check=True),
                                waits=pe_w if n3 == 1 else (),
                                inc="sP" if n3 == 16 else None)
                    mark[("projv",) + key] = S.cnt["sP"]

                def emit_body(gi):
                    prefix = gi < NP
                    t = gi if prefix else gi - NP
                    b = gi % 2
                    bx = gi % 3
                    key = keyof(gi)
                    mk_projqk = mark[("projqk",) + key]
                    mk_projv = mark[("projv",) + key]

                    if not prefix:
                        S.do("scalar", lambda e, _b=b: e.activation(
                            out=b_qh[:, _b, :], in_=pb[1][:, 0:256], func=AF.Copy,
                            scale=0.5),
                            waits=[("sP", mk_projqk), ("sV", Mg("fq", gi - 2))])
                    if prefix:
                        kh_fn = lambda e, _t=t, _b=b: e.activation(
                            out=b_kh[:, _b, :], in_=pb[0][:, 0:512], func=AF.Copy,
                            scale=c_pmh[:, _t].unsqueeze(1))
                    else:
                        kh_fn = lambda e, _b=b: e.activation(
                            out=b_kh[:, _b, :], in_=pb[0][:, 0:512], func=AF.Copy,
                            scale=0.5)
                    mark[("khdone",) + key] = S.do(
                        "scalar", kh_fn,
                        waits=[("sP", mk_projqk), ("sV", Mg("fk", gi - 2))], inc="sA")
                    mk_kh = S.cnt["sA"]
                    if not prefix:
                        mk_ktm = S.do("vector", lambda e: e.tensor_copy(
                            out=b_kb[:, :].rearrange("p (h n) -> p h n", h=16)[:, :, 0:32],
                            in_=pb[0][:, 0:512].rearrange("p (h n) -> p h n", h=16)),
                            waits=[("sP", mk_projqk), ("sA", mk_kh),
                                   ("qT", Mg("kT", gi - 1))], inc="sV")
                        mark[("ktm",) + key] = mk_ktm
                        mark[("kT",) + key] = S.do("sync", lambda e, _b=b: e.dma_start(
                            out=b_kf[:, _b, :, :],
                            in_=b_kb[:, :].rearrange("p (B i) -> p B i", B=16),
                            transpose=True),
                            waits=[("sV", mk_ktm), ("sP", Mg("state", gi - 2))], inc="qT")
                    else:
                        mark[("ktm",) + key] = 0
                    S.do("vector", lambda e, _b=b: e.tensor_copy(
                        out=b_va[:, _b, 0:8, 0:64],
                        in_=pb[2][:, 0:512].rearrange("p (h n) -> p h n", h=8)),
                        waits=[("sP", mk_projv), ("sP", Mg("state", gi - 2))])
                    S.do("vector", lambda e, _b=b: e.tensor_copy(
                        out=b_va[:, _b, 8:16, 0:64],
                        in_=pb[3][:, 0:512].rearrange("p (h n) -> p h n", h=8)))
                    if prefix:
                        ones_fn = lambda e, _b=b, _t=t: e.tensor_copy(
                            out=b_va[:, _b, :, 64].unsqueeze(2),
                            in_=c_pm1[:, _t].unsqueeze(1).unsqueeze(2).broadcast_to(
                                [128, NH, 1]))
                    else:
                        ones_fn = lambda e, _b=b: e.memset(b_va[:, _b, :, 64], 1.0)
                    mark[("vadone",) + key] = S.do("vector", ones_fn, inc="sV")
                    mk_va = S.cnt["sV"]

                    def part2():
                        if not prefix:
                            fq_src = lambda i0, ln, _b=b: b_qh[:, _b, :].rearrange(
                                "p (h i) -> p h i", h=16)[:, :, i0:i0 + ln]
                            fq_dst = lambda o, ln: b_fq[:, :, o:o + ln]
                            mk_fq = phi_ops(fq_src, fq_dst,
                                            [("sA", mk_kh), ("qT", Mg("fqT", gi - 1))],
                                            inc="sV")
                            mark[("fq",) + key] = mk_fq
                            mark[("fqT",) + key] = S.do("sync", lambda e, _b=b: e.dma_start(
                                out=b_PH[:, _b, :, :],
                                in_=b_fq[:, :, :].rearrange("p h i -> p (h i)").rearrange(
                                    "p (B i) -> p B i", B=32), transpose=True),
                                waits=[("sV", mk_fq), ("sP", Mg("state", gi - 2))],
                                inc="qT")
                        fk_src = lambda i0, ln, _b=b: b_kh[:, _b, :].rearrange(
                            "p (h i) -> p h i", h=16)[:, :, i0:i0 + ln]
                        fk_dst = lambda o, ln, _b=b: b_fk[:, _b, :, o:o + ln]
                        phi_ops(fk_src, fk_dst,
                                [("sA", mk_kh), ("sP", Mg("state", gi - 2))])
                        if prefix:
                            mk_fk = S.do("vector", lambda e, _b=b, _t=t: e.tensor_copy(
                                out=b_fk[:, _b, :, COL_ONES].unsqueeze(2),
                                in_=c_pm1[:, _t].unsqueeze(1).unsqueeze(2).broadcast_to(
                                    [128, NH, 1])), inc="sV")
                        else:
                            mk_fk = S.do("vector", lambda e, _b=b: e.memset(
                                b_fk[:, _b, :, COL_ONES], 1.0), inc="sV")
                        mark[("fk",) + key] = mk_fk

                        if not prefix:
                            S.do("vector", lambda e: e.tensor_copy(
                                out=KV1b[:, :, :], in_=KV1[:, :, :]),
                                waits=[("sP", Mg("Ylast", gi - 1))])
                            mk_kvb = S.do("vector", lambda e: e.tensor_copy(
                                out=KV2b[:, :, :], in_=KV2[:, :, :]), inc="sV")
                            mk_kT = mark[("kT",) + key]
                            mk_fqT = mark[("fqT",) + key]

                            def emit_score(h):
                                w = [("sV", M(("poly", t, h - 2), 0))]
                                if h == 0:
                                    w += [("qT", mk_kT), ("qT", mk_fqT),
                                          ("sV", Mg("stadd", gi - 1))]
                                S.do("tensor", lambda e, _b=b, _h=h: e.matmul(
                                    out=pb[4 + _h % 2][:, 0:128],
                                    lhsT=b_kf[0:16, _b, _h, :],
                                    rhs=b_PH[0:16, _b, 2 * _h, :],
                                    start=True, stop=True, tile_position=(0, 0),
                                    skip_group_check=True),
                                    waits=w, inc="sP")
                                mark[("score", t, h)] = S.cnt["sP"]
                                S.do("scalar", lambda e, _h=h: e.activation(
                                    out=b_ps[:, _h % 2, :], in_=pb[4 + _h % 2][:, 0:128],
                                    func=AF.Square, scale=c_pc[:, 0].unsqueeze(1),
                                    bias=c_pc[:, 1].unsqueeze(1)),
                                    waits=[("sP", mark[("score", t, h)]),
                                           ("sV", M(("poly", t, h - 2), 0))], inc="sA")
                                mark[("asq", t, h)] = S.cnt["sA"]
                                S.do("vector", lambda e, _h=h: e.scalar_tensor_tensor(
                                    out=b_A[:, _h % 2, :], in0=b_ps[:, _h % 2, :],
                                    scalar=0.5, in1=c_cm[:, :],
                                    op0=ALU.add, op1=ALU.mult),
                                    waits=[("sA", mark[("asq", t, h)]),
                                           ("sP", M(("Ymm", t, h - 2), 0))], inc="sV")
                                mark[("poly", t, h)] = S.cnt["sV"]

                            def emit_Y(h):
                                wy = [("sV", mark[("poly", t, h)]),
                                      ("sV", M(("norm", t, h - 2), 0))]
                                if h == 0:
                                    wy += [("sV", mk_kvb), ("sV", mk_va)]
                                S.do("tensor", lambda e, _b=b, _h=h: e.matmul(
                                    out=pb[6 + _h % 2][:, 0:65],
                                    lhsT=b_PH[:, _b, 2 * _h, :],
                                    rhs=KV1b[:, _h, :], start=True, stop=False,
                                    skip_group_check=True), waits=wy)
                                S.do("tensor", lambda e, _b=b, _h=h: e.matmul(
                                    out=pb[6 + _h % 2][:, 0:65],
                                    lhsT=b_PH[0:32, _b, 2 * _h + 1, :],
                                    rhs=KV2b[:, _h, :], start=False, stop=False,
                                    skip_group_check=True))
                                S.do("tensor", lambda e, _b=b, _h=h: e.matmul(
                                    out=pb[6 + _h % 2][:, 0:65], lhsT=b_A[:, _h % 2, :],
                                    rhs=b_va[:, _b, _h, :], start=False, stop=True,
                                    skip_group_check=True), inc="sP")
                                mark[("Ymm", t, h)] = S.cnt["sP"]
                                wn = [("sP", mark[("Ymm", t, h)])]
                                if h == 0:
                                    wn.append(("qT", Mg("yT", gi - 1)))
                                mkn1 = S.do("vector", lambda e, _h=h: e.tensor_scalar(
                                    out=b_st[:, 0, 10].unsqueeze(1),
                                    in0=pb[6 + _h % 2][:, 64].unsqueeze(1),
                                    scalar1=EPS_AT, scalar2=None, op0=ALU.add),
                                    waits=wn, inc="sV")
                                mkn2 = S.do("vector", lambda e: e.reciprocal(
                                    out=b_st[:, 0, 11].unsqueeze(1),
                                    in_=b_st[:, 0, 10].unsqueeze(1)),
                                    waits=[("sV", mkn1)], inc="sV")
                                S.do("vector", lambda e, _h=h: e.tensor_scalar(
                                    out=b_y[:, _h * 64:(_h + 1) * 64],
                                    in0=pb[6 + _h % 2][:, 0:64],
                                    scalar1=b_st[:, 0, 11].unsqueeze(1), scalar2=None,
                                    op0=ALU.mult), waits=[("sV", mkn2)], inc="sV")
                                mark[("norm", t, h)] = S.cnt["sV"]

                            for h in range(NH):
                                emit_score(h)
                                if h >= 1:
                                    emit_Y(h - 1)
                            emit_Y(NH - 1)
                            mark[("Ylast",) + key] = mark[("Ymm", t, NH - 1)]

                        for bi, (h0, h1) in enumerate(((0, 7), (7, 14), (14, 16))):
                            wst = [("sV", mk_fk), ("sV", mk_va)]
                            if not prefix:
                                wst.append(("sV", M(("poly", t, NH - 1), 0)))
                            if bi > 0:
                                wst.append(("sV", M(("staddb",) + key + (bi - 1,), 0)))
                            else:
                                wst.append(("sV", Mg("stadd", gi - 1)))
                            for h in range(h0, h1):
                                S.do("tensor", lambda e, _b=b, _h=h, _h0=h0: e.matmul(
                                    out=pb[4][:, (_h - _h0) * 65:(_h - _h0 + 1) * 65],
                                    lhsT=b_fk[:, _b, _h, 0:128], rhs=b_va[:, _b, _h, :],
                                    start=True, stop=True, skip_group_check=True),
                                    waits=(wst if h == h0 else ()))
                                S.do("tensor", lambda e, _b=b, _h=h, _h0=h0: e.matmul(
                                    out=pb[5][0:32, (_h - _h0) * 65:(_h - _h0 + 1) * 65],
                                    lhsT=b_fk[:, _b, _h, 128:160],
                                    rhs=b_va[:, _b, _h, :],
                                    start=True, stop=True, skip_group_check=True),
                                    inc="sP" if h == h1 - 1 else None)
                            mkb = S.cnt["sP"]
                            n = h1 - h0
                            wv = [("sP", mkb)]
                            if gi == 0 and bi == 0:
                                wv.append(("sV", mk_kvinit))
                            S.do("vector", lambda e, _h0=h0, _n=n: e.tensor_tensor(
                                out=KV1[:, _h0:_h0 + _n, :], in0=KV1[:, _h0:_h0 + _n, :],
                                in1=pb[4][:, 0:_n * 65].rearrange("p (h n) -> p h n", h=_n),
                                op=ALU.add), waits=wv)
                            S.do("vector", lambda e, _h0=h0, _n=n: e.tensor_tensor(
                                out=KV2[:, _h0:_h0 + _n, :], in0=KV2[:, _h0:_h0 + _n, :],
                                in1=pb[5][0:32, 0:_n * 65].rearrange("p (h n) -> p h n", h=_n),
                                op=ALU.add), inc="sV")
                            mark[("staddb",) + key + (bi,)] = S.cnt["sV"]
                        mark[("state",) + key] = mkb
                        mark[("stadd",) + key] = S.cnt["sV"]

                        if not prefix:
                            mark[("yT",) + key] = S.do("sync", lambda e: e.dma_start(
                                out=b_yf[:, :, :],
                                in_=b_y[:, :].rearrange("p (B i) -> p B i", B=8),
                                transpose=True),
                                waits=[("sV", mark[("norm", t, NH - 1)]),
                                       ("sP", Mg("wo", gi - 1))], inc="qT")
                            mk_yT = S.cnt["qT"]
                            n2 = 0
                            for c in range(8):
                                for hf in range(2):
                                    n2 += 1
                                    S.do("tensor", lambda e, _c=c, _hf=hf: e.matmul(
                                        out=pb[2 + _hf][:, 0:512], lhsT=b_yf[:, _c, :],
                                        rhs=w_o[:, _c, _hf * 512:(_hf + 1) * 512],
                                        start=(_c == 0), stop=(_c == 7),
                                        skip_group_check=True),
                                        waits=[("qT", mk_yT), ("sV", mk_va)]
                                              if n2 == 1 else (),
                                        inc="sP" if n2 == 16 else None)
                            mark[("wo",) + key] = S.cnt["sP"]
                            for hf in range(2):
                                S.do("vector", lambda e, _b=bx, _hf=hf, _t=t:
                                     e.tensor_tensor(
                                         out=x2[:, _t, _hf * 512:(_hf + 1) * 512],
                                         in0=b_x[:, _b, _hf * 512:(_hf + 1) * 512],
                                         in1=pb[2 + _hf][:, 0:512], op=ALU.add),
                                     waits=[("sP", mark[("wo",) + key])]
                                           if hf == 0 else (),
                                     inc="sV" if hf == 1 else None)
                            mark[("x2add",) + key] = S.cnt["sV"]
                        mark[("xdone",) + key] = S.cnt["sV"]
                    return part2

                emit_frontA(0)
                emit_frontT(0)
                emit_frontP(0)
                emit_frontA(1)
                for gi in range(NTOT):
                    part2 = emit_body(gi)
                    if gi + 2 < NTOT:
                        emit_frontA(gi + 2)
                    if gi + 1 < NTOT:
                        emit_frontT(gi + 1)
                    part2()
                    if gi + 1 < NTOT:
                        emit_frontP(gi + 1)
                        if gi + 1 >= NP:
                            emit_vproj(gi + 1)

                attn_end = {s: S.cnt[s] for s in S.cnt}

                if debug:
                    for t in range(NT):
                        S.do("scalar", lambda e, _t=t: e.dma_start(
                            out=d_dx2.rearrange("(n p) m -> n p m", p=128)[_t],
                            in_=x2[:, _t, :]),
                            waits=[("sV", mark[("x2add", "m", t)])], inc="qX")

            sbr = lambda name, shape, dt: top.enter_context(
                nc.sbuf_tensor(name, shape, dt, side="right"))
            nf = sbr("nf", [128, 8, TC], BF16)
            nfv = nf[:, :, :].rearrange("p c (n i) -> p c n i", i=128)
            w_g = sbr("w_g", [128, 8, 16, 128], BF16)
            w_u = sbr("w_u", [128, 8, 16, 128], BF16)

            with contextlib.ExitStack() as pf:
                sbp = lambda name, shape, dt: pf.enter_context(nc.sbuf_tensor(name, shape, dt))
                b_l2 = sbp("b_l2", [128, 2, H], F32)
                b_nn = sbp("b_nn", [128, 2, H], F32)
                b_a0 = sbp("b_a0", [128, 2, H], F32)
                b_hi = sbp("b_hi", [128, 2, H], BF16)
                b_lo = sbp("b_lo", [128, 2, H], BF16)
                b_lof = sbp("b_lof", [128, 2, 8, 128], BF16)

                for r in ("scalar", "vector", "tensor", "sync"):
                    for s in ("sP", "sV", "sA", "qT"):
                        S.wait(r, s, attn_end[s])

                for t in range(NT):
                    b = t % 2
                    mkS = emit_ln_stats(lambda hv, _t=t: x2[:, _t, hv * 512:(hv + 1) * 512], b)
                    S.wait("vector", "sA", mkS)
                    mk_l2 = S.do("vector", lambda e, _b=b, _t=t: e.tensor_scalar(
                        out=b_l2[:, _b, :], in0=x2[:, _t, :],
                        scalar1=b_st[:, _b, 4].unsqueeze(1), scalar2=b_st[:, _b, 5].unsqueeze(1),
                        op0=ALU.subtract, op1=ALU.mult),
                        waits=[("sA", M(("s2", t - 2), 0))], inc="sV")
                    mk_a0 = S.do("vector", lambda e, _b=b, _t=t: e.tensor_tensor(
                        out=b_a0[:, _b, :], in0=x2[:, _t, :], in1=b_l2[:, _b, :], op=ALU.add),
                        waits=[("qX", M(("a0w", t - 2), 0)), ("sV", mk_l2)], inc="sV")
                    mark[("a0w", t)] = S.do("scalar", lambda e, _b=b, _t=t: e.dma_start(
                        out=d_out.rearrange("(n p) m -> n p m", p=128)[_t], in_=b_a0[:, _b, :]),
                        waits=[("sV", mk_a0)], inc="qX")
                    S.wait("scalar", "sV", mk_l2)
                    mkS2 = emit_ln_stats(lambda hv, _b=b: b_l2[:, _b, hv * 512:(hv + 1) * 512], b)
                    mark[("s2", t)] = mkS2
                    S.wait("vector", "sA", mkS2)
                    mk_nn = S.do("vector", lambda e, _b=b: e.tensor_scalar(
                        out=b_nn[:, _b, :], in0=b_l2[:, _b, :],
                        scalar1=b_st[:, _b, 4].unsqueeze(1), scalar2=b_st[:, _b, 5].unsqueeze(1),
                        op0=ALU.subtract, op1=ALU.mult), inc="sV")
                    mk_hi = S.do("vector", lambda e, _b=b: e.tensor_copy(
                        out=b_hi[:, _b, :], in_=b_nn[:, _b, :]),
                        waits=[("qT", M(("hiT", t - 2), 0)), ("sV", mk_nn)], inc="sV")
                    mk_hilo = S.do("vector", lambda e, _b=b: e.scalar_tensor_tensor(
                        out=b_lo[:, _b, :], in0=b_hi[:, _b, :], scalar=-1.0,
                        in1=b_nn[:, _b, :], op0=ALU.mult, op1=ALU.add),
                        waits=[("qT", M(("loT", t - 2), 0)), ("sV", mk_hi)], inc="sV")
                    mark[("hiT", t)] = S.do("sync", lambda e, _b=b, _t=t: e.dma_start(
                        out=nfv[:, :, _t, :],
                        in_=b_hi[:, _b, :].rearrange("p (B i) -> p B i", B=8), transpose=True),
                        waits=[("sV", mk_hilo)], inc="qT")
                    mark[("loT", t)] = S.do("sync", lambda e, _b=b: e.dma_start(
                        out=b_lof[:, _b, :, :],
                        in_=b_lo[:, _b, :].rearrange("p (B i) -> p B i", B=8), transpose=True),
                        waits=[("sP", M(("lgmm", t - 2), 0))], inc="qT")
                    mk_loT = S.cnt["qT"]

                    n3 = 0
                    for (lv, rv) in (("hi", rw_h), ("hi", rw_l), ("lo", rw_h)):
                        for c in range(8):
                            n3 += 1
                            if lv == "hi":
                                lf = lambda _c=c, _t=t: nfv[:, _c, _t, :]
                            else:
                                lf = lambda _c=c, _b=b: b_lof[:, _b, _c, :]
                            w = []
                            if n3 == 1:
                                w = [("qT", mk_loT), ("sV", M(("wrt", t - 1), 0))]
                            S.do("tensor", lambda e, _lf=lf, _r=rv, _c=c, _n=n3:
                                 e.matmul(out=pb[1][:, 0:8], lhsT=_lf(), rhs=_r[:, _c, :],
                                          start=(_n == 1), stop=False, skip_group_check=True),
                                 waits=w)
                    S.do("tensor", lambda e: e.matmul(
                        out=pb[1][:, 0:8], lhsT=c_one1[:, :], rhs=brw_h[0:1, :],
                        start=False, stop=False, skip_group_check=True),
                        waits=[("sV", mk_brw)] if t == 0 else ())
                    mark[("lgmm", t)] = S.do("tensor", lambda e: e.matmul(
                        out=pb[1][:, 0:8], lhsT=c_one1[:, :], rhs=brw_l[0:1, :],
                        start=False, stop=True, skip_group_check=True), inc="sP")
                    mk_rm = S.do("vector", lambda e, _t=t: e.reduce_max(
                        out=b_mx[:, _t].unsqueeze(1), in_=pb[1][:, 0:8], axis=AX),
                        waits=[("sP", mark[("lgmm", t)])], inc="sV")
                    mark[("wrt", t)] = S.do("vector", lambda e, _t=t: e.tensor_tensor(
                        out=b_eq[:, _t, :], in0=pb[1][:, 0:8],
                        in1=b_mx[:, _t].unsqueeze(1).broadcast_to([128, 8]), op=ALU.is_equal),
                        waits=[("sV", mk_rm)], inc="sV")

                S.wait("scalar", "sV", mark[("wrt", NT - 1)])
                mk_sg = S.do("scalar", lambda e: e.activation(
                    out=b_mx[:, :], in_=b_mx[:, :], func=AF.Sigmoid), inc="sA")
                for t in range(NT):
                    S.do("vector", lambda e, _t=t: e.tensor_scalar(
                        out=w_rt[:, 0:8, _t], in0=b_eq[:, _t, :],
                        scalar1=b_mx[:, _t].unsqueeze(1), scalar2=None, op0=ALU.mult),
                        waits=[("sA", mk_sg)] if t == 0 else (),
                        inc="sV" if t == NT - 1 else None)
                mk_wrtall = S.cnt["sV"]

                S.wait("vector", "qT", mark[("hiT", NT - 1)])
                S.wait("vector", "sP", mark[("lgmm", NT - 1)])
                for c in range(8):
                    S.do("vector", lambda e, _c=c: e.tensor_scalar(
                        out=nf[:, _c, :], in0=nf[:, _c, :],
                        scalar1=lnwb[:, 0, _c].unsqueeze(1), scalar2=lnwb[:, 1, _c].unsqueeze(1),
                        op0=ALU.mult, op1=ALU.add),
                        inc="sV" if c == 7 else None)
                mk_nf = S.cnt["sV"]
                preffn_end = {s: S.cnt[s] for s in S.cnt}

                if debug:
                    for c in range(8):
                        S.do("gpsimd", lambda e, _c=c: e.dma_start(
                            out=d_dnf.rearrange("(c p) n -> c p n", p=128)[_c],
                            in_=nf[:, _c, :]), waits=[("sV", mk_nf)], inc="qW")
                    S.do("scalar", lambda e: e.dma_start(
                        out=d_dwr, in_=w_rt[:, :, :].rearrange("p e t -> p (e t)")),
                        waits=[("sV", mk_wrtall)], inc="qX")


        if phases == "full":
            with contextlib.ExitStack() as ff:
                sbf = lambda name, shape, dt: ff.enter_context(nc.sbuf_tensor(name, shape, dt))
                w_d = sbf("w_d", [128, 16, 2, 512], BF16)
                hm_tm = sbf("hm_tm", [128, 4, IO], BF16)
                hm_io = sbf("hm_io", [128, 16, 512], BF16)
                b_sv = sbf("b_sv", [128, 2, 512], F32)
                dtemp = sbf("dtemp", [128, 2, 4, H], F32)

                S.wait("gpsimd", "sP", attn_end["sP"])
                S.wait("gpsimd", "sV", attn_end["sV"])
                S.wait("gpsimd", "sA", attn_end["sA"])
                S.wait("gpsimd", "qT", attn_end["qT"])
                for r in ("tensor", "vector", "scalar", "sync"):
                    S.wait(r, "sV", mk_nf)
                    S.wait(r, "sV", mk_wrtall)
                    S.wait(r, "qX", preffn_end["qX"])
                    S.wait(r, "qT", preffn_end["qT"])

                def lin(e_i, g, tl):
                    return (e_i * NG + g) * 4 + tl

                def emit_gu_phase(e_i, g):
                    base_step = (e_i * NG + g) * 16
                    j_outer = (g == NG - 1)
                    order = ([(tl, j) for j in range(4) for tl in range(4)] if j_outer
                             else [(tl, j) for tl in range(4) for j in range(4)])
                    for si, (tl, j) in enumerate(order):
                        st = base_step + si
                        T = 4 * g + tl
                        L = lin(e_i, g, tl)
                        sl = st % 2
                        wgu = [("qW", mark[("gld", e_i, j)]),
                               ("qW", mark[("uld", e_i, j)]),
                               ("sV", M(("hmS", st - 2), 0))]
                        for c in range(8):
                            S.do("tensor", lambda e, _c=c, _T=T, _j=j, _sl=sl:
                                 e.matmul(
                                     out=pb[_sl][:, 0:512],
                                     lhsT=nfv[:, _c, _T, :],
                                     rhs=w_g[:, _c, 4 * _j:4 * _j + 4, :].rearrange(
                                         "p j n -> p (j n)"),
                                     start=(_c == 0), stop=(_c == 7),
                                     skip_group_check=True),
                                 waits=wgu if c == 0 else ())
                            S.do("tensor", lambda e, _c=c, _T=T, _j=j, _sl=sl:
                                 e.matmul(
                                     out=pb[2 + _sl][:, 0:512],
                                     lhsT=nfv[:, _c, _T, :],
                                     rhs=w_u[:, _c, 4 * _j:4 * _j + 4, :].rearrange(
                                         "p j n -> p (j n)"),
                                     start=(_c == 0), stop=(_c == 7),
                                     skip_group_check=True),
                                 inc="sP" if c == 7 else None)
                        mark[("gu", L, j)] = S.cnt["sP"]
                        mark[("guq", e_i, j)] = S.cnt["sP"]
                        mark[("sv", L, j)] = S.do("scalar",
                            lambda e, _e=e_i, _T=T, _sl=sl: e.activation(
                                out=b_sv[:, _sl, :], in_=pb[_sl][:, 0:512],
                                func=AF.Silu, scale=w_rt[:, _e, _T].unsqueeze(1)),
                            waits=[("sP", mark[("gu", L, j)]),
                                   ("sV", M(("hmS", st - 2), 0))], inc="sA")
                        m = S.do("vector",
                            lambda e, _e=e_i, _T=T, _j=j, _sl=sl, _tl=tl:
                            e.scalar_tensor_tensor(
                                out=hm_tm[:, _tl, 512 * _j:512 * (_j + 1)],
                                in0=pb[2 + _sl][:, 0:512],
                                scalar=w_rt[:, _e, _T].unsqueeze(1),
                                in1=b_sv[:, _sl, :], op0=ALU.mult, op1=ALU.mult),
                            waits=[("sA", mark[("sv", L, j)]),
                                   ("qT", M(("hmT", L - 4), 0))], inc="sV")
                        mark[("hmS", st)] = m
                        mark[("hm", L, j)] = m
                    if g == NG - 1:
                        mark[("guend", e_i)] = S.cnt["sP"]

                def emit_hmT(e_i, g):
                    eg = e_i * NG + g
                    for tl in range(4):
                        L = lin(e_i, g, tl)
                        dprev = M(("dnmm", ((eg - 1) * 4 + tl) * 2 + 1), 0)
                        mark[("hmT", L)] = S.do("sync", lambda e, _sl=tl, _tl=tl:
                            e.dma_start(
                                out=hm_io[:, :, :].rearrange(
                                    "p j (n i) -> p j n i", i=128)[:, :, _tl, :],
                                in_=hm_tm[:, _sl, :].rearrange("p (B i) -> p B i", B=16),
                                transpose=True),
                            waits=[("sV", mark[("hm", L, 3)]), ("sP", dprev)],
                            inc="qT")

                def emit_down(e_i, g):
                    eg = e_i * NG + g
                    for tl in range(4):
                        L = lin(e_i, g, tl)
                        for hf in range(2):
                            bank = 4 + (2 * tl + hf) % 4
                            D = (eg * 4 + tl) * 2 + hf
                            wdn = [("qT", mark[("hmT", L)]),
                                   ("qW", mark[("dld", e_i, hf)]),
                                   ("sV", M(("dnev", D - 4), 0))]
                            for j in range(16):
                                S.do("tensor", lambda e, _j=j, _tl=tl, _hf=hf, _bk=bank:
                                     e.matmul(
                                         out=pb[_bk][:, 0:512],
                                         lhsT=hm_io[:, _j, :].rearrange(
                                             "p (n i) -> p n i", i=128)[:, _tl, :],
                                         rhs=w_d[:, _j, _hf, :],
                                         start=(_j == 0), stop=(_j == 15),
                                         skip_group_check=True),
                                     waits=wdn if j == 0 else (),
                                     inc="sP" if j == 15 else None)
                            mark[("dnmm", D)] = S.cnt["sP"]
                            mark[("dnev", D)] = S.do("vector",
                                lambda e, _tl=tl, _hf=hf, _bk=bank, _g=g: e.tensor_copy(
                                    out=dtemp[:, _g % 2, _tl, _hf * 512:(_hf + 1) * 512],
                                    in_=pb[_bk][:, 0:512]),
                                waits=[("sP", mark[("dnmm", D)]),
                                       ("qW", M(("accq", eg - 2), 0))],
                                inc="sV")
                    Dlast = (eg * 4 + 3) * 2 + 1
                    mark[("dn", e_i, g)] = mark[("dnmm", Dlast)]
                    if g == NG - 1:
                        mark[("dend", e_i)] = mark[("dn", e_i, g)]
                    mark[("accq", eg)] = S.do("gpsimd",
                        lambda e, _g=g: e.dma_start(
                            out=d_out.rearrange("(G T p) h -> G p T h", p=128, T=4)[_g],
                            in_=dtemp[:, _g % 2, :, :], accum_op=ALU.add),
                        waits=[("sV", mark[("dnev", Dlast)]),
                               ("qX", mark[("a0w", 4 * g + 3)])], inc="qW")

                def emit_dld(e_i, first):
                    dld_w = [("sP", M(("dend", e_i - 1), 0))]
                    if first:
                        dld_w += [("sV", preffn_end["sV"]), ("qT", preffn_end["qT"]),
                                  ("sP", preffn_end["sP"]), ("sA", preffn_end["sA"]),
                                  ("qX", preffn_end["qX"])]
                    for s_ in range(2):
                        mark[("dld", e_i, s_)] = S.do("gpsimd", lambda e, _e=e_i, _s=s_:
                            e.dma_start(
                                out=w_d[:, :, _s, :],
                                in_=d_dw[_e][:, 512 * _s:512 * (_s + 1)].rearrange(
                                    "(j p) n -> p j n", p=128)),
                            waits=dld_w if s_ == 0 else (), inc="qW")

                pending = None
                for e_i in range(NE):
                    for q in range(4):
                        mark[("gld", e_i, q)] = S.do("gpsimd", lambda e, _e=e_i, _q=q:
                            e.dma_start(
                                out=w_g[:, :, 4 * _q:4 * _q + 4, :],
                                in_=d_gw[_e][:, 512 * _q:512 * (_q + 1)].rearrange(
                                    "(c p) (j n) -> p c j n", p=128, n=128)),
                            waits=[("sP", M(("guq", e_i - 1, q), 0))], inc="qW")
                        mark[("uld", e_i, q)] = S.do("gpsimd", lambda e, _e=e_i, _q=q:
                            e.dma_start(
                                out=w_u[:, :, 4 * _q:4 * _q + 4, :],
                                in_=d_uw[_e][:, 512 * _q:512 * (_q + 1)].rearrange(
                                    "(c p) (j n) -> p c j n", p=128, n=128)),
                            inc="qW")
                    if e_i == 0:
                        emit_dld(0, first=True)
                    for g in range(NG):
                        emit_gu_phase(e_i, g)
                        if pending is not None:
                            emit_down(*pending)
                            if pending == (e_i - 1, NG - 1):
                                emit_dld(e_i, first=False)
                        emit_hmT(e_i, g)
                        pending = (e_i, g)
                emit_down(*pending)

                S.wait("gpsimd", "qW", S.cnt["qW"])
                S.wait("gpsimd", "qX", S.cnt["qX"])
        else:
            S.wait("gpsimd", "qX", S.cnt["qX"])
            S.wait("gpsimd", "sV", S.cnt["sV"])

        with (
            nc.semaphore("qX") as h_qx,
            nc.semaphore("qT") as h_qt,
            nc.semaphore("qW") as h_qw,
            nc.semaphore("sP") as h_sp,
            nc.semaphore("sV") as h_sv,
            nc.semaphore("sA") as h_sa,
            nc.Block() as block,
        ):
            S.bind({"qX": h_qx, "qT": h_qt, "qW": h_qw,
                    "sP": h_sp, "sV": h_sv, "sA": h_sa})

            @block.sync
            def _(eng):
                for g, fn in S.ops["sync"]:
                    if _TRUNC is None or g < _TRUNC:
                        fn(eng)

            @block.scalar
            def _(eng):
                for g, fn in S.ops["scalar"]:
                    if _TRUNC is None or g < _TRUNC:
                        fn(eng)

            @block.gpsimd
            def _(eng):
                for g, fn in S.ops["gpsimd"]:
                    if _TRUNC is None or g < _TRUNC:
                        fn(eng)

            @block.tensor
            def _(eng):
                for g, fn in S.ops["tensor"]:
                    if _TRUNC is None or g < _TRUNC:
                        fn(eng)

            @block.vector
            def _(eng):
                for g, fn in S.ops["vector"]:
                    if _TRUNC is None or g < _TRUNC:
                        fn(eng)

    return nc


def _prep_inputs(inputs):
    x = np.asarray(inputs["L0_attn_hidden_states"], np.float32)
    Wq = np.asarray(inputs["L0_attn_Wq"], np.float32)
    Wk = np.asarray(inputs["L0_attn_Wk"], np.float32)
    Wv = np.asarray(inputs["L0_attn_Wv"], np.float32)
    Wo = np.asarray(inputs["L0_attn_Wo"], np.float32)
    lnw = np.asarray(inputs["L0_ffn_ln_w"], np.float32)
    lnb = np.asarray(inputs["L0_ffn_ln_b"], np.float32)
    rw = np.asarray(inputs["L0_ffn_router_w"], np.float32)
    gw = np.asarray(inputs["L0_ffn_gate_w"], np.float32)
    uw = np.asarray(inputs["L0_ffn_up_w"], np.float32)
    dw = np.asarray(inputs["L0_ffn_down_w"], np.float32)
    shg = np.asarray(inputs["L0_ffn_sh_gate"], np.float32)
    shu = np.asarray(inputs["L0_ffn_sh_up"], np.float32)
    shd = np.asarray(inputs["L0_ffn_sh_down"], np.float32)

    wqT = np.ascontiguousarray(Wq.T)
    wkT_pad = np.zeros((H, 512), np.float32)
    for h in range(NH):
        wkT_pad[:, 32 * h:32 * h + 16] = Wk.T[:, 16 * h:16 * h + 16]
    wvT = np.ascontiguousarray(Wv.T)
    woT = np.ascontiguousarray(Wo.T)
    rwm = np.ascontiguousarray(rw.reshape(8, 128, 8).transpose(1, 0, 2).reshape(128, 64))
    lnw_m = np.ascontiguousarray(lnw.reshape(8, 128).T)
    lnb_m = np.ascontiguousarray(lnb.reshape(8, 128).T)
    cm = np.ascontiguousarray(
        (np.arange(128)[:, None] <= np.arange(128)[None, :]).astype(np.float32))
    gw9 = np.ascontiguousarray(np.concatenate([gw, shg[None]], axis=0))
    uw9 = np.ascontiguousarray(np.concatenate([uw, shu[None]], axis=0))
    dw9 = np.ascontiguousarray(np.concatenate([dw, shd[None]], axis=0))

    in_maps = []
    for c in range(NCORE):
        batch, chunk = divmod(c, 4)
        t0 = chunk * TC
        xm = np.ascontiguousarray(x[batch, t0:t0 + TC, :])
        xp = np.zeros((NP * 128, H), np.float32)
        npre = min(t0, NP * 128)
        xp[:npre] = x[batch, :npre, :]
        valid = (np.arange(NP * 128) < npre)
        pm = np.ascontiguousarray(valid.reshape(NP, 128).T.astype(np.float32))
        in_maps.append({
            "x_main": xm, "x_pre": xp,
            "pmask_h": np.ascontiguousarray(0.5 * pm), "pmask_1": pm,
            "cmask": cm, "wqT": wqT, "wkT": wkT_pad, "wvT": wvT, "woT": woT,
            "rwm": rwm, "lnw": lnw_m, "lnb": lnb_m,
            "gw": gw9, "uw": uw9, "dw": dw9,
        })
    return in_maps


_cache = {}


def kernel(**inputs):
    key = (_PHASES, _DEBUG)
    if key not in _cache:
        _cache[key] = build_program(debug=_DEBUG, phases=_PHASES)
    nc = _cache[key]
    in_maps = _prep_inputs(inputs)
    res = run_bass_kernel_spmd(nc, in_maps, core_ids=list(range(NCORE)))
    outs = [np.asarray(res.results[c]["out"]) for c in range(NCORE)]
    full = np.stack([np.concatenate(outs[0:4], axis=0),
                     np.concatenate(outs[4:8], axis=0)], axis=0)
    if _DEBUG:
        kernel.debug = res.results
    return full
